# version 14
# baseline (speedup 1.0000x reference)
import numpy as np
import ml_dtypes
import concourse.bass as bass
import concourse.mybir as mybir
from concourse.bass_utils import run_bass_kernel_spmd

F32 = mybir.dt.float32
BF16 = mybir.dt.bfloat16
I8 = mybir.dt.int8
AF = mybir.ActivationFunctionType
ALU = mybir.AluOpType

B_LOC = 2
T = 2048
C = 2048
NCC = 16
NTT = 4
NTC = 16
EPS1 = 1e-5
EPSG = 0.00064
NCORES = 8

ENGINES = ("sync", "scalar", "vector", "tensor", "gpsimd")

DEBUG = False
N_BATCH_EMIT = 2
FORCE_DRAIN = False


class Prog:

    def __init__(self):
        self.ops = {e: [] for e in ENGINES}
        self.count = {}
        self.producer = {}
        self.step = {}
        self.hwm = {e: {} for e in ENGINES}

    def chan(self, name, eng, step):
        self.count[name] = 0
        self.producer[name] = eng
        self.step[name] = step
        return name

    def dchan(self, name):
        if name not in self.count:
            self.count[name] = 0
            self.producer[name] = "dma"
            self.step[name] = 16
        return name

    def emit(self, eng, build, waits=(), inc=None, dr=False):
        wl = []
        for ch, v in waits:
            assert v <= self.count[ch], (eng, ch, v, self.count[ch])
            if v <= 0:
                continue
            if self.producer[ch] == eng and self.step[ch] == 1:
                continue
            if self.hwm[eng].get(ch, 0) >= v:
                continue
            self.hwm[eng][ch] = v
            wl.append((ch, v))
        if inc is not None:
            self.count[inc] += self.step[inc]
        self.ops[eng].append((build, wl, inc, dr))

    def val(self, ch):
        return self.count[ch]


class Slot:
    __slots__ = ("free",)

    def __init__(self):
        self.free = []


def build_program(debug=False):
    nc = bass.Bass("TRN2", target_bir_lowering=False)
    dt = {}

    def din(name, shape, dtype):
        dt[name] = nc.dram_tensor(name, list(shape), dtype, kind="ExternalInput")
        return dt[name]

    x_fm = din("x_fm", (B_LOC, NCC, 128, T), F32)
    w_r = din("w_r", (16, 128, 16, 128), F32)
    w_k = din("w_k", (16, 128, 16, 128), F32)
    w_g = din("w_g", (16, 128, 16, 128), F32)
    w_o = din("w_o", (16, 128, 16, 128), F32)
    w_v = din("w_v", (4, 128, 16, 512), F32)
    w_gate = din("w_gate", (64, 128, 16, 128), I8)
    w_up = din("w_up", (64, 128, 16, 128), I8)
    w_down = din("w_down", (16, 128, 64, 128), I8)
    smalls = din("smalls", (128, 176), F32)
    sgu = din("sgu", (128, 128), F32)
    wkv0 = din("wkv0", (32, 64, 64), F32)
    iota_d = din("iota_d", (128, T), F32)
    ones_cb = din("ones_cb", (128, 1), BF16)
    ones_rf = din("ones_rf", (1, 128), F32)
    maskS = din("maskS", (128, 128), F32)
    b2c = din("b2c", (128, 2), F32)

    out_fm = nc.dram_tensor("out_fm", [B_LOC, NCC, 128, T], F32, kind="ExternalOutput")
    dt["out_fm"] = out_fm

    def dscr(name, shape, dtype):
        dt[name] = nc.dram_tensor(name, list(shape), dtype, kind="Internal")
        return dt[name]

    r_scr = dscr("r_scr", (16, 128, T), BF16)
    kw_scr = dscr("kw_scr", (16, 128, T), BF16)
    g_scr = dscr("g_scr", (16, 128, T), BF16)
    v_scr = dscr("v_scr", (16, NTC, 128, 128), BF16)
    yn_scr = dscr("yn_scr", (16, NTC, 128, 128), BF16)
    x1_scr = dscr("x1_scr", (NCC, 128, T), F32)

    dbg = {}
    if debug:
        for nm, shp in [("dbg_ln", (NCC, 128, T)), ("dbg_sT", (128, NTC, 32)),
                        ("dbg_yn", (16, NTC, 128, 128)), ("dbg_x1", (NCC, 128, T)),
                        ("dbg_ln2", (NCC, 128, T)), ("dbg_r", (16, 128, T)), ("dbg_z", (NCC, 128, T)),
                        ("dbg_fuse", (64, 128, 512)),
                        ("dbg_kw", (16, 128, T)), ("dbg_v", (16, NTC, 128, 128)),
                        ("dbg_g", (16, 128, T)), ("dbg_ys", (128, NTC, 128))]:
            dbg[nm] = nc.dram_tensor(nm, list(shp), F32, kind="ExternalOutput")

    ara_off, _ = nc.bump_sbuf(65536)

    def at(name, shape, dtype, off):
        return nc.alloc_sbuf_tensor_at(name, list(shape), dtype, offset=ara_off + off)

    ARA = at("ARA", (128, 32768), BF16, 0)
    XT = [at("XT0", (128, T), F32, 0), at("XT1", (128, T), F32, 8192)]
    XB = [at("XB0", (128, T), BF16, 16384), at("XB1", (128, T), BF16, 20480)]
    XSQ = [at("XSQ0", (128, T), BF16, 24576), at("XSQ1", (128, T), BF16, 28672)]
    ROWS = at("ROWS", (1, 3 * T), F32, 32768)
    RT = [at("RT0", (128, T), BF16, 0), at("RT1", (128, T), BF16, 4096)]
    KW = [at("KW0", (128, T), BF16, 8192), at("KW1", (128, T), BF16, 12288)]
    KC = [at("KC0", (128, NTC, 128), BF16, 16384), at("KC1", (128, NTC, 128), BF16, 20480)]
    VS = [at("VS0", (128, NTC, 128), BF16, 24576), at("VS1", (128, NTC, 128), BF16, 28672)]
    WFOR = at("WFOR", (128, T), BF16, 32768)
    RWB = at("RWB", (128, T), BF16, 36864)
    YS = at("YS", (128, NTC, 128), BF16, 40960)
    YN = at("YN", (128, NTC, 128), BF16, 45056)
    ATC = [at("ATC0", (128, 128), BF16, 49152), at("ATC1", (128, 128), BF16, 49408)]

    LN = nc.alloc_sbuf_tensor("LN", [128, NCC, T + 1], BF16)
    WB = [nc.alloc_sbuf_tensor(f"WB{i}", [128, 8192], BF16) for i in range(2)]
    mub_off, _ = nc.bump_sbuf(16384)

    def at2(name, shape, dtype, off):
        return nc.alloc_sbuf_tensor_at(name, list(shape), dtype, offset=mub_off + off)

    MUB = at2("MUB", (128, 2 * T), F32, 0)
    GBUF = [at2("GB0", (128, T), BF16, 0), at2("GB1", (128, T), BF16, 4096)]
    DTB = nc.alloc_sbuf_tensor("DTB", [128, T], BF16)
    STG = [nc.alloc_sbuf_tensor(f"STG{i}", [128, 512], F32) for i in range(4)]
    SBF = [nc.alloc_sbuf_tensor(f"SBF{i}", [128, 512], BF16) for i in range(6)]
    IOT = nc.alloc_sbuf_tensor("IOT", [128, T], F32)
    SM = nc.alloc_sbuf_tensor("SM", [128, 256], F32)
    SGU = nc.alloc_sbuf_tensor("SGU", [128, 128], F32)
    STT = nc.alloc_sbuf_tensor("STT", [128, NTC * 32], BF16)
    MK = nc.alloc_sbuf_tensor("MK", [128, 128], F32)
    BNS = nc.alloc_sbuf_tensor("BNS", [128, 192], F32)
    GNC = nc.alloc_sbuf_tensor("GNC", [128, 160], F32)
    SSB = nc.alloc_sbuf_tensor("SSB", [128, 64], BF16)
    S0B = nc.alloc_sbuf_tensor("S0B", [128, 64], BF16)
    ONESF = nc.alloc_sbuf_tensor("ONESF", [1, 128], F32)
    ONESB = nc.alloc_sbuf_tensor("ONESB", [128, 1], BF16)
    B2F = nc.alloc_sbuf_tensor("B2F", [128, 2], F32)
    B2U = nc.alloc_sbuf_tensor("B2U", [128, 16, 2], BF16)

    assert nc.sbuf_bytes_remaining >= 0, nc.sbuf_bytes_remaining

    pb = [nc.alloc_psum_tensor(f"pb{i}", [128, 512], F32) for i in range(8)]

    p = Prog()
    pe = p.chan("pe", "tensor", 1)
    dbgc = p.chan("dbgc", "gpsimd", 16)
    dve = p.chan("dve", "vector", 1)
    act = p.chan("act", "scalar", 1)
    ldc = p.dchan("ldc")
    for _n in ("xb0", "xb1", "xt0", "xt1", "wb0", "wb1", "rrd", "se0", "se1",
               "art0", "art1", "akw0", "akw1", "avs0", "avs1", "akc0", "akc1",
               "as0", "syn", "zt", "gb0", "gb1", "sx0", "sx1", "sxs0", "sxs1",
               "so0", "so1"):
        p.dchan(_n)

    s_xt = [Slot(), Slot()]
    s_xb = [Slot(), Slot()]
    s_xsq = [Slot(), Slot()]
    s_wb = [Slot(), Slot()]
    s_ara = Slot()
    s_ln = Slot()
    s_dtb = Slot()
    s_mub = Slot()
    s_stg = [Slot() for _ in range(4)]
    s_sbf = [Slot() for _ in range(6)]
    s_rt = [Slot(), Slot()]
    s_kw = [Slot(), Slot()]
    s_kc = [Slot(), Slot()]
    s_vs = [Slot(), Slot()]
    s_gb = [Slot(), Slot()]
    s_yn_store = [0]
    pbf = [[] for _ in range(8)]

    p.emit("sync", lambda e: e.dma_start(SM[:, 0:176], smalls[:, :]), inc=ldc)
    p.emit("sync", lambda e: e.dma_start(SGU[:, :], sgu[:, :]), inc=ldc)
    p.emit("sync", lambda e: e.dma_start(MK[:, :], maskS[:, :]), inc=ldc)
    p.emit("sync", lambda e: e.dma_start(IOT[:, :], iota_d[:, :]), inc=ldc)
    p.emit("sync", lambda e: e.dma_start(ONESF[:, :], ones_rf[:, :]), inc=ldc)
    p.emit("sync", lambda e: e.dma_start(ONESB[:, :], ones_cb[:, :]), inc=ldc)
    p.emit("sync", lambda e: e.dma_start(B2F[:, :], b2c[:, :]), inc=ldc)
    v_setup = p.val(ldc)
    p.emit("scalar", lambda e: e.activation(SM[:, 176:192], SM[:, 64:80], AF.Exp),
           waits=[(ldc, v_setup)], inc=act, dr=True)
    v_e = p.val(act)
    p.emit("vector", lambda e: e.tensor_scalar_mul(SM[:, 192:208], SM[:, 176:192], -(T - 1.0)),
           waits=[(act, v_e)], inc=dve, dr=True)
    p.emit("vector", lambda e: e.tensor_scalar_mul(SM[:, 208:224], SM[:, 176:192], -1.0), inc=dve, dr=True)
    for fc in range(16):
        p.emit("vector", lambda e, fc=fc: e.tensor_scalar_mul(
            B2U[:, fc, :], B2F[:, :], SM[:, 144 + fc:145 + fc]),
            waits=[(ldc, v_setup)], inc=dve, dr=True)
    p.emit("scalar", lambda e: e.activation(SM[:, 226:227], SM[:, 64:65], AF.Silu),
           waits=[(ldc, v_setup)], inc=act, dr=True)
    p.emit("vector", lambda e: e.memset(SM[:, 224:225], EPS1), inc=dve, dr=True)
    p.emit("vector", lambda e: e.memset(SM[:, 225:226], EPSG), inc=dve, dr=True)
    p.emit("vector", lambda e: e.tensor_copy(
        LN[:, :, 0:1], SM[:, 128:144].rearrange("p (c o) -> p c o", o=1)),
        waits=[(ldc, v_setup)], inc=dve, dr=True)
    v_smalls = p.val(dve)

    def emit_ln(src_ap, eps_ap):
        guard0 = list(s_ara.free)
        pe_tail = [0, 0]
        for cc in range(NCC):
            xb, xq = XB[cc % 2], XSQ[cc % 2]
            slb, slq = s_xb[cc % 2], s_xsq[cc % 2]
            ch = p.dchan(f"xb{cc % 2}")
            p.emit("gpsimd", lambda e, cc=cc, xb=xb: e.dma_start(xb[:, :], src_ap(cc)),
                   waits=(guard0 if cc < 2 else []) + slb.free, inc=ch)
            v_ld = p.val(ch)
            p.emit("scalar", lambda e, xb=xb, xq=xq: e.activation(xq[:, :], xb[:, :], AF.Square),
                   waits=[(ch, v_ld)] + slq.free, inc=act, dr=True)
            v_sq = p.val(act)
            for tt in range(NTT):
                p.emit("tensor", lambda e, xb=xb, tt=tt, cc=cc: e.matmul(
                    pb[tt][0:1, :], ONESB[:, :], xb[:, tt * 512:(tt + 1) * 512],
                    start=(cc == 0), stop=(cc == NCC - 1), skip_group_check=True),
                    waits=[(ch, v_ld), (ldc, v_setup)] + (pbf[tt] if cc == 0 else []), inc=None)
                p.emit("tensor", lambda e, xq=xq, tt=tt, cc=cc: e.matmul(
                    pb[4 + tt][0:1, :], ONESB[:, :], xq[:, tt * 512:(tt + 1) * 512],
                    start=(cc == 0), stop=(cc == NCC - 1), skip_group_check=True),
                    waits=[(act, v_sq)] + (pbf[4 + tt] if cc == 0 else []),
                    inc=(pe if tt == NTT - 1 else None))
            slb.free = [(pe, p.val(pe))]
            slq.free = [(pe, p.val(pe))]
        v_stats = p.val(pe)
        for tt in range(NTT):
            p.emit("vector", lambda e, tt=tt: e.tensor_copy(ROWS[0:1, tt * 512:(tt + 1) * 512], pb[tt][0:1, :]),
                   waits=[(pe, v_stats)], inc=dve, dr=(tt == NTT - 1))
            p.emit("vector", lambda e, tt=tt: e.tensor_copy(ROWS[0:1, T + tt * 512:T + (tt + 1) * 512], pb[4 + tt][0:1, :]),
                   inc=dve, dr=(tt == NTT - 1))
        p.emit("vector", lambda e: e.tensor_scalar_mul(ROWS[0:1, 0:T], ROWS[0:1, 0:T], 1.0 / C), inc=dve, dr=True)
        p.emit("vector", lambda e: e.tensor_scalar_mul(ROWS[0:1, T:2 * T], ROWS[0:1, T:2 * T], 1.0 / C), inc=dve, dr=True)
        p.emit("vector", lambda e: e.tensor_tensor(ROWS[0:1, 2 * T:3 * T], ROWS[0:1, 0:T], ROWS[0:1, 0:T], ALU.mult), inc=dve, dr=True)
        p.emit("vector", lambda e: e.tensor_tensor(ROWS[0:1, T:2 * T], ROWS[0:1, T:2 * T], ROWS[0:1, 2 * T:3 * T], ALU.subtract), inc=dve, dr=True)
        p.emit("vector", lambda e: e.tensor_scalar_max(ROWS[0:1, T:2 * T], ROWS[0:1, T:2 * T], 0.0), inc=dve, dr=True)
        v_var = p.val(dve)
        p.emit("scalar", lambda e: e.activation(ROWS[0:1, T:2 * T], ROWS[0:1, T:2 * T], AF.Ln, bias=eps_ap),
               waits=[(dve, v_var)], inc=act, dr=True)
        p.emit("scalar", lambda e: e.activation(ROWS[0:1, T:2 * T], ROWS[0:1, T:2 * T], AF.Exp, scale=-0.5), inc=act, dr=True)
        v_rstd = p.val(act)
        p.emit("vector", lambda e: e.tensor_scalar_mul(ROWS[0:1, 0:T], ROWS[0:1, 0:T], -1.0),
               waits=[(act, v_rstd)], inc=dve, dr=True)
        v_rows = p.val(dve)
        for tt in range(NTT):
            p.emit("tensor", lambda e, tt=tt: e.matmul(
                pb[tt][:, :], ONESF[:, :], ROWS[0:1, tt * 512:(tt + 1) * 512], start=True, stop=True),
                waits=[(dve, v_rows)], inc=None)
            p.emit("tensor", lambda e, tt=tt: e.matmul(
                pb[4 + tt][:, :], ONESF[:, :], ROWS[0:1, T + tt * 512:T + (tt + 1) * 512], start=True, stop=True),
                waits=[(act, v_rstd)], inc=(pe if tt == NTT - 1 else None))
        v_bc = p.val(pe)
        for tt in range(NTT):
            p.emit("vector", lambda e, tt=tt: e.tensor_copy(MUB[:, tt * 512:(tt + 1) * 512], pb[tt][:, :]),
                   waits=[(pe, v_bc)] + (s_mub.free if tt == 0 else []), inc=dve)
            p.emit("vector", lambda e, tt=tt: e.tensor_copy(MUB[:, T + tt * 512:T + (tt + 1) * 512], pb[4 + tt][:, :]),
                   inc=dve, dr=(tt == NTT - 1))
        v_mub = p.val(dve)
        for i in range(8):
            pbf[i] = [(dve, v_mub)]
        for cc in range(NCC):
            xt = XT[cc % 2]
            sl = s_xt[cc % 2]
            ch = p.dchan(f"xt{cc % 2}")
            p.emit("sync", lambda e, cc=cc, xt=xt: e.dma_start(xt[:, :], src_ap(cc)),
                   waits=sl.free + (guard0 if cc < 2 else []), inc=ch)
            v_ld = p.val(ch)
            p.emit("vector", lambda e, xt=xt: e.tensor_tensor(xt[:, :], xt[:, :], MUB[:, 0:T], ALU.add),
                   waits=[(ch, v_ld), (dve, v_mub)] + (s_ln.free if cc == 0 else []), inc=dve, dr=True)
            p.emit("vector", lambda e, xt=xt, cc=cc: e.tensor_tensor(
                LN[:, cc, 1:T + 1], xt[:, :], MUB[:, T:2 * T], ALU.mult), inc=dve)
            sl.free = [(dve, p.val(dve))]
        s_mub.free = [(dve, p.val(dve))]
        return p.val(dve)

    for b in range(N_BATCH_EMIT):
        v_lnd = emit_ln(lambda cc, b=b: x_fm[b, cc, :, :], SM[0:1, 224:225])
        v_lnd = p.val(dve)
        if debug and b == 0:
            for cc in range(NCC):
                p.emit("gpsimd", lambda e, cc=cc: e.dma_start(dbg["dbg_ln"][cc, :, :], LN[:, cc, 1:T + 1]),
                       waits=[(dve, v_lnd)], inc=dbgc)

        def emit_mix(a_off):
            g0 = list(s_ara.free) + list(s_dtb.free)
            vcc = []
            for cc in range(NCC):
                p.emit("vector", lambda e, cc=cc: e.tensor_tensor(
                    DTB[:, :], LN[:, cc, 1:T + 1], LN[:, cc, 0:T], ALU.subtract),
                    waits=(g0 if cc == 0 else []), inc=dve, dr=True)
                p.emit("vector", lambda e, cc=cc, ao=a_off: e.scalar_tensor_tensor(
                    ARA[:, cc * 2048:(cc + 1) * 2048], DTB[:, :], SM[:, ao + cc:ao + cc + 1],
                    LN[:, cc, 0:T], ALU.mult, ALU.add), inc=dve, dr=True)
                vcc.append(p.val(dve))
            s_dtb.free = [(dve, p.val(dve))]
            return vcc

        def fm_proj(wdram, evac, v_mix_cc):
            for o in range(16):
                wb_ = WB[o % 2]
                slw = s_wb[o % 2]
                chw = p.dchan(f"wb{o % 2}")
                p.emit("gpsimd", lambda e, o=o, wb_=wb_, w=wdram: e.dma_start(
                    wb_[:, 0:2048], w[o].rearrange("p cc o -> p (cc o)")),
                    waits=slw.free, inc=chw)
                v_w = p.val(chw)
                for tt in range(NTT):
                    bi = (o * NTT + tt) % 2
                    bank = pb[bi]
                    for cc in range(NCC):
                        p.emit("tensor", lambda e, cc=cc, tt=tt, wb_=wb_, bank=bank: e.matmul(
                            bank[:, :], wb_[:, cc * 128:(cc + 1) * 128],
                            ARA[:, cc * 2048 + tt * 512: cc * 2048 + (tt + 1) * 512],
                            start=(cc == 0), stop=(cc == NCC - 1), skip_group_check=True),
                            waits=[(chw, v_w), (dve, v_mix_cc[cc])] + (pbf[bi] if cc == 0 else []),
                            inc=(pe if cc == NCC - 1 else None))
                    evac(o, tt, bank, p.val(pe), bi)
                slw.free = [(pe, p.val(pe))]
            s_ara.free = [(pe, p.val(pe))]

        def evac_r(o, tt, bank, v_pe, bi):
            st = SBF[tt % 2]
            sl = s_sbf[tt % 2]
            p.emit("vector", lambda e, bank=bank, st=st: e.tensor_copy(st[:, :], bank[:, :]),
                   waits=[(pe, v_pe)] + sl.free, inc=dve, dr=True)
            pbf[bi] = [(dve, p.val(dve))]
            v_c = p.val(dve)
            che = p.dchan(f"se{tt % 2}")
            p.emit("scalar", lambda e, o=o, tt=tt, st=st: e.dma_start(
                r_scr[o, :, tt * 512:(tt + 1) * 512], st[:, :]), waits=[(dve, v_c)], inc=che)
            sl.free = [(che, p.val(che))]

        vmx = emit_mix(32)
        fm_proj(w_r, evac_r, vmx)
        v_r_stored = [("se0", p.val("se0")), ("se1", p.val("se1"))]

        def evac_k(o, tt, bank, v_pe, bi):
            wbs = STG[2]
            slb = s_stg[2]
            p.emit("scalar", lambda e, o=o, tt=tt: e.activation(
                STG[2][:, :], IOT[:, tt * 512:(tt + 1) * 512], AF.Exp,
                scale=SM[:, 176 + o:177 + o], bias=SM[:, 192 + o:193 + o]),
                waits=slb.free + [(dve, v_smalls)], inc=act, dr=True)
            v_wb = p.val(act)
            rt_ = SBF[4]
            slr = s_sbf[4]
            chr_ = p.dchan("rrd")
            p.emit("sync", lambda e, o=o, tt=tt: e.dma_start(
                SBF[4][:, :], r_scr[o, :, tt * 512:(tt + 1) * 512]),
                waits=slr.free + v_r_stored, inc=chr_)
            v_rt = p.val(chr_)
            st = SBF[tt % 2]
            sl = s_sbf[tt % 2]
            p.emit("vector", lambda e, bank=bank, st=st: e.tensor_tensor(
                st[:, :], bank[:, :], STG[2][:, :], ALU.mult),
                waits=[(pe, v_pe), (act, v_wb)] + sl.free, inc=dve, dr=True)
            v_kw = p.val(dve)
            slb.free = [(dve, v_kw)]
            che = p.dchan(f"se{tt % 2}")
            p.emit("scalar", lambda e, o=o, tt=tt, st=st: e.dma_start(
                kw_scr[o, :, tt * 512:(tt + 1) * 512], st[:, :]), waits=[(dve, v_kw)], inc=che)
            sl.free = [(che, p.val(che))]
            pr = SBF[5]
            slp = s_sbf[5]
            p.emit("vector", lambda e, bank=bank, pr=pr: e.tensor_tensor(
                pr[:, :], bank[:, :], SBF[4][:, :], ALU.mult),
                waits=[(chr_, v_rt)] + slp.free, inc=dve, dr=True)
            v_pr = p.val(dve)
            slr.free = [(dve, v_pr)]
            pbf[bi] = [(dve, v_pr)]
            for ts in range(4):
                col = (tt * 4 + ts) * 2
                p.emit("tensor", lambda e, ts=ts, col=col, pr=pr, o=o: e.matmul(
                    pb[7][:, col:col + 2], pr[:, ts * 128:(ts + 1) * 128], B2U[:, o, :],
                    start=True, stop=True, skip_group_check=True),
                    waits=[(dve, v_pr), (dve, v_smalls)] + (pbf[7] if (tt == 0 and ts == 0) else []),
                    inc=(pe if ts == 3 else None))
            slp.free = [(pe, p.val(pe))]
            if tt == NTT - 1:
                v_s = p.val(pe)
                p.emit("vector", lambda e, o=o: e.tensor_copy(
                    STT[:, :].rearrange("p (tc h) -> p tc h", h=32)[:, :, 2 * o:2 * o + 2],
                    pb[7][:, 0:32].rearrange("p (tc h) -> p tc h", h=2)),
                    waits=[(pe, v_s)], inc=dve, dr=True)
                pbf[7] = [(dve, p.val(dve))]

        vmx = emit_mix(0)
        fm_proj(w_k, evac_k, vmx)

        def evac_g(o, tt, bank, v_pe, bi):
            st = SBF[tt % 2]
            sl = s_sbf[tt % 2]
            p.emit("scalar", lambda e, bank=bank, st=st: e.activation(st[:, :], bank[:, :], AF.Silu),
                   waits=[(pe, v_pe)] + sl.free, inc=act, dr=True)
            pbf[bi] = [(act, p.val(act))]
            v_c = p.val(act)
            che = p.dchan(f"se{tt % 2}")
            p.emit("scalar", lambda e, o=o, tt=tt, st=st: e.dma_start(
                g_scr[o, :, tt * 512:(tt + 1) * 512], st[:, :]), waits=[(act, v_c)], inc=che)
            sl.free = [(che, p.val(che))]

        vmx = emit_mix(48)
        fm_proj(w_g, evac_g, vmx)

        vmxv = emit_mix(16)
        for oc in range(4):
            wb_ = WB[oc % 2]
            slw = s_wb[oc % 2]
            chw = p.dchan(f"wb{oc % 2}")
            p.emit("gpsimd", lambda e, oc=oc, wb_=wb_: e.dma_start(
                wb_[:, :], w_v[oc].rearrange("p cc o -> p (cc o)")),
                waits=slw.free, inc=chw)
            v_w = p.val(chw)
            for tc in range(NTC):
                bi = (oc * NTC + tc) % 2
                bank = pb[bi]
                for cc in range(NCC):
                    p.emit("tensor", lambda e, cc=cc, tc=tc, wb_=wb_, bank=bank: e.matmul(
                        bank[:, :], ARA[:, cc * 2048 + tc * 128: cc * 2048 + (tc + 1) * 128],
                        wb_[:, cc * 512:(cc + 1) * 512],
                        start=(cc == 0), stop=(cc == NCC - 1), skip_group_check=True),
                        waits=[(chw, v_w), (dve, vmxv[cc])] + (pbf[bi] if cc == 0 else []),
                        inc=(pe if cc == NCC - 1 else None))
                v_pe = p.val(pe)
                st = SBF[tc % 2]
                sl = s_sbf[tc % 2]
                p.emit("vector", lambda e, bank=bank, st=st: e.tensor_copy(st[:, :], bank[:, :]),
                       waits=[(pe, v_pe)] + sl.free, inc=dve, dr=True)
                pbf[bi] = [(dve, p.val(dve))]
                v_c = p.val(dve)
                che = p.dchan(f"se{tc % 2}")
                for j in range(4):
                    p.emit("scalar", lambda e, oc=oc, tc=tc, j=j, st=st: e.dma_start(
                        v_scr[oc * 4 + j, tc, :, :], st[:, j * 128:(j + 1) * 128]),
                        waits=[(dve, v_c)], inc=che)
                sl.free = [(che, p.val(che))]
            slw.free = [(pe, p.val(pe))]
        s_ara.free = [(pe, p.val(pe))]
        v_proj_done = [("se0", p.val("se0")), ("se1", p.val("se1"))]
        if debug and b == 0:
            p.emit("gpsimd", lambda e: e.dma_start(dbg["dbg_r"][:, :, :], r_scr[:, :, :]),
                   waits=v_proj_done, inc=dbgc)
            p.emit("gpsimd", lambda e: e.dma_start(dbg["dbg_kw"][:, :, :], kw_scr[:, :, :]),
                   waits=v_proj_done, inc=dbgc)
            p.emit("gpsimd", lambda e: e.dma_start(dbg["dbg_v"][:, :, :, :], v_scr[:, :, :, :]),
                   waits=v_proj_done, inc=dbgc)
            p.emit("gpsimd", lambda e: e.dma_start(dbg["dbg_g"][:, :, :], g_scr[:, :, :]),
                   waits=v_proj_done, inc=dbgc)

        atc_free = [[], []]
        s_copy = [0, 0]
        y_evac = [0, 0]
        s_s0 = Slot()
        s_wfor = Slot()
        s_rw = Slot()
        s_ssb = Slot()
        for sl in (s_rt + s_kw + s_kc + s_vs + s_gb):
            sl.free = list(s_ara.free)
        v_ys_guard = 0
        for fc in range(16):
            rt_, kw_, kc_, vs_ = RT[fc % 2], KW[fc % 2], KC[fc % 2], VS[fc % 2]
            slr, slk, slc, slv = s_rt[fc % 2], s_kw[fc % 2], s_kc[fc % 2], s_vs[fc % 2]
            chrt = p.dchan(f"art{fc % 2}")
            chkw = p.dchan(f"akw{fc % 2}")
            chvs = p.dchan(f"avs{fc % 2}")
            chkc = p.dchan(f"akc{fc % 2}")
            chs0 = p.dchan("as0")
            p.emit("sync", lambda e, fc=fc, d=rt_: e.dma_start(d[:, :], r_scr[fc, :, :]),
                   waits=v_proj_done + slr.free, inc=chrt)
            v_r = p.val(chrt)
            p.emit("sync", lambda e, fc=fc, d=kw_: e.dma_start(d[:, :], kw_scr[fc, :, :]),
                   waits=slk.free, inc=chkw)
            v_kw = p.val(chkw)
            p.emit("sync", lambda e, fc=fc, d=vs_: e.dma_start(
                d[:, :, :], v_scr[fc].rearrange("tc p j -> p tc j")),
                waits=slv.free, inc=chvs)
            v_v = p.val(chvs)
            for tc in range(NTC):
                p.emit("scalar", lambda e, fc=fc, tc=tc, d=kc_: e.dma_start(
                    d[:, tc, :], kw_scr[fc, :, tc * 128:(tc + 1) * 128], transpose=True),
                    waits=((v_proj_done + slc.free) if tc == 0 else []), inc=chkc)
            v_kc = p.val(chkc)
            p.emit("gpsimd", lambda e, fc=fc: e.dma_start(S0B[0:64, :], wkv0[2 * fc, :, :]),
                   waits=s_s0.free, inc=chs0)
            p.emit("gpsimd", lambda e, fc=fc: e.dma_start(S0B[64:128, :], wkv0[2 * fc + 1, :, :]), inc=chs0)
            v_s0 = p.val(chs0)
            p.emit("scalar", lambda e, fc=fc: e.activation(
                WFOR[:, :], IOT[:, :], AF.Exp, scale=SM[:, 208 + fc:209 + fc]),
                waits=s_wfor.free + [(dve, v_smalls)], inc=act, dr=True)
            v_wf = p.val(act)
            p.emit("vector", lambda e, d=rt_: e.tensor_tensor(RWB[:, :], d[:, :], WFOR[:, :], ALU.mult),
                   waits=[(chrt, v_r), (act, v_wf)] + s_rw.free, inc=dve, dr=True)
            v_rw = p.val(dve)
            s_wfor.free = [(dve, v_rw)]
            p.emit("vector", lambda e: e.memset(SSB[:, :], 0.0), waits=s_ssb.free, inc=dve, dr=True)
            v_sz = p.val(dve)

            for tc in range(NTC):
                for hh in range(2):
                    hs = slice(64 * hh, 64 * hh + 64)
                    bank_a = pb[hh]
                    p.emit("tensor", lambda e, hh=hh, tc=tc, hs=hs, bank=bank_a, d=kw_, r=rt_: e.matmul(
                        bank[:, 0:128], d[hs, tc * 128:(tc + 1) * 128], r[hs, tc * 128:(tc + 1) * 128],
                        start=True, stop=True, skip_group_check=True),
                        waits=[(chkw, v_kw), (chrt, v_r)] + atc_free[hh] +
                              (pbf[hh] if tc == 0 else []), inc=pe)
                    v_att = p.val(pe)
                    atc = ATC[hh]
                    p.emit("vector", lambda e, bank=bank_a, atc=atc: e.tensor_tensor(
                        atc[:, :], bank[:, 0:128], MK[:, :], ALU.mult),
                        waits=[(pe, v_att)], inc=dve, dr=True)
                    v_atc = p.val(dve)
                    atc_free[hh] = [(pe, 0)]
                    bank_o = pb[2 + hh]
                    oreg = bank_o[:, 0:64]
                    p.emit("tensor", lambda e, atc=atc, tc=tc, hh=hh, v=vs_, oreg=oreg: e.matmul(
                        oreg, atc[:, :], v[:, tc, hs_c(hh)], start=True, stop=False,
                        skip_group_check=True),
                        waits=[(dve, v_atc), (chvs, v_v)] +
                              ([(dve, y_evac[hh])] if y_evac[hh] else []) +
                              (pbf[2 + hh] if tc == 0 else []), inc=None)
                    if tc > 0:
                        p.emit("tensor", lambda e, r=rt_, hs=hs, tc=tc, oreg=oreg, hh=hh: e.matmul(
                            oreg, r[hs, tc * 128:(tc + 1) * 128], SSB[hs, :],
                            start=False, stop=False, tile_position=(64 * hh, 0),
                            skip_group_check=True),
                            waits=[(act, s_copy[hh])], inc=None)
                    p.emit("tensor", lambda e, hs=hs, tc=tc, oreg=oreg, hh=hh: e.matmul(
                        oreg, RWB[hs, tc * 128:(tc + 1) * 128], S0B[hs, :],
                        start=False, stop=True, tile_position=(64 * hh, 0),
                        skip_group_check=True),
                        waits=[(dve, v_rw), (chs0, v_s0)], inc=pe)
                    v_out = p.val(pe)
                    atc_free[hh] = [(pe, v_out)]
                    bank_s = pb[6 + hh]
                    p.emit("tensor", lambda e, d=kc_, tc=tc, hh=hh, v=vs_, bank=bank_s, hs=hs: e.matmul(
                        bank[hs, 0:64], d[:, tc, hs_c(hh)], v[:, tc, hs_c(hh)],
                        start=(tc == 0), stop=(tc == NTC - 1), tile_position=(0, 64 * hh),
                        skip_group_check=True),
                        waits=[(chkc, v_kc)] +
                              ([(dve, v_sz)] if tc == 0 else [(act, s_copy[hh])]) +
                              (pbf[6 + hh] if tc == 0 else []), inc=pe)
                    v_su = p.val(pe)
                    if tc < NTC - 1:
                        p.emit("scalar", lambda e, bank=bank_s, hs=hs: e.copy(SSB[hs, :], bank[hs, 0:64]),
                               waits=[(pe, v_su)], inc=act, dr=True)
                        s_copy[hh] = p.val(act)
                    p.emit("vector", lambda e, v=vs_, tc=tc, hh=hh, fc=fc, oreg=oreg: e.scalar_tensor_tensor(
                        YS[:, tc, hs_c(hh)], v[:, tc, hs_c(hh)],
                        STT[:, :].rearrange("p (tc h) -> p tc h", h=32)[:, tc, 2 * fc + hh:2 * fc + hh + 1],
                        oreg, ALU.mult, ALU.add),
                        waits=[(pe, v_out)] +
                              ([("syn", s_yn_store[0])] if (tc == 0 and hh == 0) else []), inc=dve,
                        dr=(tc == NTC - 1 and hh == 1))
                    y_evac[hh] = p.val(dve)
            v_scan_pe = p.val(pe)
            v_scan = p.val(dve)
            s_ssb.free = [(pe, v_scan_pe)]
            s_rw.free = [(pe, v_scan_pe)]
            s_s0.free = [(pe, v_scan_pe)]
            slr.free = [(pe, v_scan_pe), (dve, v_rw)]
            slk.free = [(pe, v_scan_pe)]
            slc.free = [(pe, v_scan_pe)]
            slv.free = [(dve, v_scan), (pe, v_scan_pe)]
            pbf[0] = [(dve, v_scan)]
            pbf[1] = [(dve, v_scan)]
            pbf[2] = [(dve, v_scan)]
            pbf[3] = [(dve, v_scan)]
            pbf[6] = [(act, s_copy[0]), (dve, v_scan)]
            pbf[7] = [(act, s_copy[1]), (dve, v_scan)]
            s_copy = [0, 0]
            y_evac = [0, 0]

            for tc in range(NTC):
                for hh in range(2):
                    g = tc * 2 + hh
                    p.emit("vector", lambda e, tc=tc, hh=hh, g=g: e.bn_stats(
                        BNS[:, g * 6:(g + 1) * 6], YS[:, tc, hs_c(hh)]), inc=dve,
                        dr=(g == 31))
            bm1 = BNS[:, 1:192:6]
            bm2 = BNS[:, 4:192:6]
            bq1 = BNS[:, 2:192:6]
            bq2 = BNS[:, 5:192:6]
            p.emit("vector", lambda e: e.tensor_tensor(GNC[:, 0:32], bm1, bm2, ALU.add), inc=dve)
            p.emit("vector", lambda e: e.tensor_tensor(GNC[:, 128:160], bm1, bm2, ALU.subtract), inc=dve)
            p.emit("vector", lambda e: e.tensor_scalar_mul(GNC[:, 0:32], GNC[:, 0:32], 0.5), inc=dve)
            p.emit("vector", lambda e: e.tensor_scalar_mul(GNC[:, 128:160], GNC[:, 128:160], 0.5), inc=dve)
            p.emit("vector", lambda e: e.tensor_tensor(GNC[:, 32:64], bq1, bq2, ALU.add), inc=dve)
            p.emit("vector", lambda e: e.tensor_tensor(GNC[:, 128:160], GNC[:, 128:160], GNC[:, 128:160], ALU.mult), inc=dve)
            p.emit("vector", lambda e: e.tensor_scalar_mul(GNC[:, 32:64], GNC[:, 32:64], 1.0 / 64), inc=dve, dr=True)
            p.emit("vector", lambda e: e.tensor_tensor(GNC[:, 32:64], GNC[:, 32:64], GNC[:, 128:160], ALU.add), inc=dve, dr=True)
            p.emit("vector", lambda e: e.tensor_scalar_max(GNC[:, 32:64], GNC[:, 32:64], 0.0), inc=dve, dr=True)
            v_gn = p.val(dve)
            p.emit("scalar", lambda e: e.activation(GNC[:, 64:96], GNC[:, 32:64], AF.Ln, bias=SM[:, 225:226]),
                   waits=[(dve, v_gn)], inc=act, dr=True)
            p.emit("scalar", lambda e: e.activation(GNC[:, 64:96], GNC[:, 64:96], AF.Exp, scale=-0.5), inc=act, dr=True)
            v_rstd = p.val(act)
            p.emit("vector", lambda e: e.tensor_tensor(GNC[:, 96:128], GNC[:, 0:32], GNC[:, 64:96], ALU.mult),
                   waits=[(act, v_rstd)], inc=dve, dr=True)
            p.emit("vector", lambda e: e.tensor_scalar_mul(GNC[:, 96:128], GNC[:, 96:128], -1.0), inc=dve, dr=True)
            v_gnb = p.val(dve)
            for tc in range(NTC):
                for hh in range(2):
                    g = tc * 2 + hh
                    p.emit("vector", lambda e, tc=tc, hh=hh, g=g: e.tensor_scalar(
                        YN[:, tc, hs_c(hh)], YS[:, tc, hs_c(hh)],
                        GNC[:, 64 + g:65 + g], GNC[:, 96 + g:97 + g], ALU.mult, ALU.add),
                        waits=([("syn", s_yn_store[0])] if (tc == 0 and hh == 0) else []), inc=dve,
                        dr=(tc == NTC - 1 and hh == 1))
            v_yn = p.val(dve)
            p.emit("scalar", lambda e, fc=fc: e.dma_start(
                yn_scr[fc].rearrange("tc p j -> p tc j"), YN[:, :, :]),
                waits=[(dve, v_yn)], inc="syn")
            s_yn_store[0] = p.val("syn")
            if debug and b == 0 and fc == 0:
                p.emit("gpsimd", lambda e: e.dma_start(dbg["dbg_ys"][:, :, :], YS[:, :, :]),
                       waits=[(dve, v_yn)], inc=dbgc)
                pass
        s_ara.free = [(pe, p.val(pe)), (dve, p.val(dve))]
        v_A_done = p.val("syn")
        if debug and b == 0:
            p.emit("gpsimd", lambda e: e.dma_start(dbg["dbg_yn"][:, :, :, :], yn_scr[:, :, :, :]),
                   waits=[("syn", v_A_done)], inc=dbgc)
            p.emit("gpsimd", lambda e: e.dma_start(
                dbg["dbg_sT"][:, :, :], STT[:, :].rearrange("p (tc h) -> p tc h", h=32)),
                waits=[("syn", v_A_done)], inc=dbgc)

        g0 = list(s_ara.free)
        for cc in range(NCC):
            p.emit("scalar", lambda e, cc=cc: e.dma_start(
                ARA[:, cc * 2048:(cc + 1) * 2048],
                yn_scr[cc].rearrange("tc p j -> (tc p) j"), transpose=True),
                waits=([("syn", v_A_done)] + g0) if cc == 0 else [], inc="zt")
        v_z = p.val("zt")
        v_z_cc = []
        for cc in range(NCC):
            p.emit("vector", lambda e, cc=cc: e.tensor_scalar(
                ARA[:, cc * 2048:(cc + 1) * 2048], ARA[:, cc * 2048:(cc + 1) * 2048],
                SM[:, 80 + cc:81 + cc], SM[:, 96 + cc:97 + cc], ALU.mult, ALU.add),
                waits=[("zt", v_z)], inc=dve, dr=True)
            gb = GBUF[cc % 2]
            slg = s_gb[cc % 2]
            chg = p.dchan(f"gb{cc % 2}")
            p.emit("sync", lambda e, cc=cc, d=gb: e.dma_start(d[:, :], g_scr[cc, :, :]),
                   waits=slg.free + (s_mub.free if cc < 2 else []), inc=chg)
            v_g = p.val(chg)
            p.emit("vector", lambda e, cc=cc, d=gb: e.tensor_tensor(
                ARA[:, cc * 2048:(cc + 1) * 2048], ARA[:, cc * 2048:(cc + 1) * 2048],
                d[:, :], ALU.mult),
                waits=[(chg, v_g)], inc=dve, dr=True)
            v_z_cc.append(p.val(dve))
            slg.free = [(dve, p.val(dve))]
        v_zdone = p.val(dve)
        if debug and b == 0:
            for cc_ in range(NCC):
                p.emit("gpsimd", lambda e, cc_=cc_: e.dma_start(
                    dbg["dbg_z"][cc_, :, :], ARA[:, cc_ * 2048:(cc_ + 1) * 2048]),
                    waits=[(dve, v_zdone)], inc=dbgc)
        for o in range(16):
            wb_ = WB[o % 2]
            slw = s_wb[o % 2]
            chw = p.dchan(f"wb{o % 2}")
            p.emit("gpsimd", lambda e, o=o, wb_=wb_: e.dma_start(
                wb_[:, 0:2048], w_o[o].rearrange("p cc o -> p (cc o)")),
                waits=slw.free, inc=chw)
            v_w = p.val(chw)
            for tt in range(NTT):
                bi = (o * NTT + tt) % 2
                bank = pb[bi]
                for cc in range(NCC):
                    p.emit("tensor", lambda e, cc=cc, tt=tt, wb_=wb_, bank=bank: e.matmul(
                        bank[:, :], wb_[:, cc * 128:(cc + 1) * 128],
                        ARA[:, cc * 2048 + tt * 512:cc * 2048 + (tt + 1) * 512],
                        start=(cc == 0), stop=(cc == NCC - 1), skip_group_check=True),
                        waits=[(chw, v_w), (dve, v_z_cc[cc])] + (pbf[bi] if cc == 0 else []),
                        inc=(pe if cc == NCC - 1 else None))
                v_pe = p.val(pe)
                xt = STG[tt % 2]
                slx = s_stg[tt % 2]
                chx = p.dchan(f"sx{tt % 2}")
                p.emit("sync", lambda e, b=b, o=o, tt=tt, xt=xt: e.dma_start(
                    xt[:, :], x_fm[b, o, :, tt * 512:(tt + 1) * 512]),
                    waits=slx.free, inc=chx)
                v_x = p.val(chx)
                x1s = STG[2 + (tt % 2)]
                slx1 = s_stg[2 + (tt % 2)]
                p.emit("vector", lambda e, bank=bank, xt=xt, x1s=x1s: e.tensor_tensor(
                    x1s[:, :], bank[:, :], xt[:, :], ALU.add),
                    waits=[(pe, v_pe), (chx, v_x)] + slx1.free, inc=dve, dr=True)
                v_x1 = p.val(dve)
                pbf[bi] = [(dve, v_x1)]
                slx.free = [(dve, v_x1)]
                chxs = p.dchan(f"sxs{tt % 2}")
                p.emit("scalar", lambda e, o=o, tt=tt, x1s=x1s: e.dma_start(
                    x1_scr[o, :, tt * 512:(tt + 1) * 512], x1s[:, :]),
                    waits=[(dve, v_x1)], inc=chxs)
                slx1.free = [(chxs, p.val(chxs))]
            slw.free = [(pe, p.val(pe))]
        s_ara.free = [(pe, p.val(pe))]
        s_mub.free = [(dve, p.val(dve))]
        v_x1_done = [("sxs0", p.val("sxs0")), ("sxs1", p.val("sxs1"))]
        if debug and b == 0:
            p.emit("gpsimd", lambda e: e.dma_start(dbg["dbg_x1"][:, :, :], x1_scr[:, :, :]),
                   waits=v_x1_done, inc=dbgc)

        s_ln.free = [(pe, p.val(pe))]
        s_ara.free = s_ara.free + v_x1_done
        emit_ln(lambda cc: x1_scr[cc, :, :], SM[0:1, 224:225])
        v_ln2 = p.val(dve)
        if debug and b == 0:
            for cc in range(NCC):
                p.emit("gpsimd", lambda e, cc=cc: e.dma_start(dbg["dbg_ln2"][cc, :, :], LN[:, cc, 1:T + 1]),
                       waits=[(dve, v_ln2)], inc=dbgc)

        pbf2 = pbf
        for tt in range(NTT):
            for oc in range(64):
                par = oc % 2
                gv = WB[0][:, par * 2048:(par + 1) * 2048]
                uv = WB[1][:, par * 2048:(par + 1) * 2048]
                slw = s_wb[par]
                chw = p.dchan(f"wb{par}")
                p.emit("gpsimd", lambda e, oc=oc, gv=gv: e.dma_start(
                    gv, w_gate[oc].rearrange("p cc o -> p (cc o)")),
                    waits=slw.free, inc=chw)
                p.emit("gpsimd", lambda e, oc=oc, uv=uv: e.dma_start(
                    uv, w_up[oc].rearrange("p cc o -> p (cc o)")), inc=chw)
                v_w = p.val(chw)
                bg = pb[par * 2]
                bu = pb[par * 2 + 1]
                for cc in range(NCC):
                    p.emit("tensor", lambda e, cc=cc, tt=tt, gv=gv, bg=bg: e.matmul(
                        bg[:, :], gv[:, cc * 128:(cc + 1) * 128],
                        LN[:, cc, 1 + tt * 512:1 + (tt + 1) * 512],
                        start=(cc == 0), stop=(cc == NCC - 1), skip_group_check=True),
                        waits=[(chw, v_w), (dve, v_ln2)] + (pbf[par * 2] if cc == 0 else []),
                        inc=(pe if cc == NCC - 1 else None))
                    p.emit("tensor", lambda e, cc=cc, tt=tt, uv=uv, bu=bu: e.matmul(
                        bu[:, :], uv[:, cc * 128:(cc + 1) * 128],
                        LN[:, cc, 1 + tt * 512:1 + (tt + 1) * 512],
                        start=(cc == 0), stop=(cc == NCC - 1), skip_group_check=True),
                        waits=[(chw, v_w)] + (pbf[par * 2 + 1] if cc == 0 else []),
                        inc=(pe if cc == NCC - 1 else None))
                v_pe = p.val(pe)
                slw.free = [(pe, v_pe)]
                gst = SBF[par]
                slgst = s_sbf[par]
                p.emit("scalar", lambda e, bg=bg, oc=oc, gst=gst: e.activation(
                    gst[:, :], bg[:, :], AF.Silu, scale=SGU[:, oc:oc + 1]),
                    waits=[(pe, v_pe)] + slgst.free, inc=act, dr=True)
                v_gst = p.val(act)
                pbf[par * 2] = [(act, v_gst)]
                ust = SBF[2 + par]
                slust = s_sbf[2 + par]
                p.emit("vector", lambda e, bu=bu, oc=oc, ust=ust: e.tensor_scalar_mul(
                    ust[:, :], bu[:, :], SGU[:, 64 + oc:65 + oc]),
                    waits=[(pe, v_pe)] + slust.free, inc=dve, dr=True)
                v_ust = p.val(dve)
                pbf[par * 2 + 1] = [(dve, v_ust)]
                p.emit("vector", lambda e, oc=oc, gst=gst, ust=ust: e.tensor_tensor(
                    ARA[:, oc * 512:(oc + 1) * 512], gst[:, :], ust[:, :], ALU.mult),
                    waits=[(act, v_gst)] + (s_ara.free if oc == 0 else []),
                    inc=dve, dr=True)
                slgst.free = [(dve, p.val(dve))]
                slust.free = [(dve, p.val(dve))]
            v_fuse = p.val(dve)
            v_pe_gu = p.val(pe)
            if debug and b == 0 and tt == 0:
                for oc_ in range(64):
                    p.emit("gpsimd", lambda e, oc_=oc_: e.dma_start(
                        dbg["dbg_fuse"][oc_, :, :], ARA[:, oc_ * 512:(oc_ + 1) * 512]),
                        waits=[(dve, v_fuse)], inc=dbgc)
            for o2 in range(16):
                par = o2 % 2
                dwv = WB[par][:, 0:8192]
                slw = s_wb[par]
                chw = p.dchan(f"wb{par}")
                p.emit("gpsimd", lambda e, o2=o2, dwv=dwv: e.dma_start(
                    dwv, w_down[o2].rearrange("p fc o -> p (fc o)")),
                    waits=slw.free + ([(pe, v_pe_gu)] if o2 < 2 else []), inc=chw)
                v_w = p.val(chw)
                bank = pb[4 + par]
                for fc2 in range(64):
                    p.emit("tensor", lambda e, fc2=fc2, dwv=dwv, bank=bank: e.matmul(
                        bank[:, :], dwv[:, fc2 * 128:(fc2 + 1) * 128],
                        ARA[:, fc2 * 512:(fc2 + 1) * 512],
                        start=(fc2 == 0), stop=(fc2 == 63), skip_group_check=True),
                        waits=[(chw, v_w), (dve, v_fuse)] + (pbf[4 + par] if fc2 == 0 else []),
                        inc=(pe if fc2 == 63 else None))
                v_pe = p.val(pe)
                slw.free = [(pe, v_pe)]
                xt = STG[par]
                slx = s_stg[par]
                chx = p.dchan(f"sx{par}")
                p.emit("sync", lambda e, o2=o2, tt=tt, xt=xt: e.dma_start(
                    xt[:, :], x1_scr[o2, :, tt * 512:(tt + 1) * 512]),
                    waits=slx.free + v_x1_done, inc=chx)
                v_x1t = p.val(chx)
                ost = STG[2 + par]
                slo = s_stg[2 + par]
                p.emit("vector", lambda e, bank=bank, o2=o2, xt=xt, ost=ost: e.scalar_tensor_tensor(
                    ost[:, :], bank[:, :], SM[:, 112 + o2:113 + o2], xt[:, :], ALU.mult, ALU.add),
                    waits=[(pe, v_pe), (chx, v_x1t)] + slo.free, inc=dve, dr=True)
                v_o = p.val(dve)
                pbf[4 + par] = [(dve, v_o)]
                slx.free = [(dve, v_o)]
                cho = p.dchan(f"so{par}")
                p.emit("sync", lambda e, b=b, o2=o2, tt=tt, ost=ost: e.dma_start(
                    out_fm[b, o2, :, tt * 512:(tt + 1) * 512], ost[:, :]),
                    waits=[(dve, v_o)], inc=cho)
                slo.free = [(cho, p.val(cho))]
            s_ara.free = [(pe, p.val(pe))]
        s_ln.free = [(pe, p.val(pe))]

    final_so = [("so0", p.val("so0")), ("so1", p.val("so1"))]

    semref = {ch: nc.alloc_semaphore(f"sem_{ch}") for ch in p.count}
    nc.all_engine_barrier()
    nums = sorted(s.num for s in semref.values())
    lo, hi = nums[0], nums[-1] + 1
    assert nums == list(range(lo, hi)), nums
    nc.gpsimd.dma_reset(range(lo, hi)) if hasattr(nc.gpsimd, "dma_reset") else None
    nc.gpsimd.sem_clear(range(lo, hi))
    nc.all_engine_barrier()
    if True:
        with nc.Block() as blk:
            def mk(name):
                def replay(eng):
                    for build, waits, inc, dr in p.ops[name]:
                        for ch, v in waits:
                            eng.wait_ge(semref[ch], v)
                        ins = build(eng)
                        if dr or (FORCE_DRAIN and inc in ("dve", "act")):
                            ins = eng.drain(fusable=False)
                        if inc is not None:
                            ins.then_inc(semref[inc], p.step[inc])
                    if name == "sync":
                        for chn, vv in final_so:
                            eng.wait_ge(semref[chn], vv)
                    if name == "gpsimd" and p.val("dbgc"):
                        eng.wait_ge(semref["dbgc"], p.val("dbgc"))
                return replay
            blk.sync(mk("sync"))
            blk.scalar(mk("scalar"))
            blk.vector(mk("vector"))
            blk.tensor(mk("tensor"))
            blk.gpsimd(mk("gpsimd"))
    return nc


def hs_c(hh):
    return slice(64 * hh, 64 * hh + 64)



def _prep_inputs(inputs):
    x = np.asarray(inputs["L0_attn_x"])

    def colfm(v):
        return np.asarray(v, np.float32).reshape(-1).reshape(16, 128).T

    def tile_lhsT(wt, n_ocol, osz):
        wt = np.asarray(wt)
        a = wt.reshape(wt.shape[0] // 128, 128, n_ocol, osz)
        return np.ascontiguousarray(a.transpose(2, 1, 0, 3))

    smalls = np.zeros((128, 176), np.float32)
    smalls[:, 0:16] = colfm(inputs["L0_attn_time_mix_k"])
    smalls[:, 16:32] = colfm(inputs["L0_attn_time_mix_v"])
    smalls[:, 32:48] = colfm(inputs["L0_attn_time_mix_r"])
    smalls[:, 48:64] = colfm(inputs["L0_attn_time_mix_g"])
    smalls[:, 64:80] = colfm(inputs["L0_attn_time_decay"])
    smalls[:, 80:96] = colfm(inputs["L0_attn_ln_w"])
    smalls[:, 96:112] = colfm(inputs["L0_attn_ln_b"])
    smalls[:, 112:128] = colfm(inputs["L0_ffn_s_down"])
    smalls[:, 128:144] = colfm(inputs["L0_attn_state"])
    smalls[:, 144:160] = colfm(inputs["L0_attn_time_faaaa"])
    sgu = np.zeros((128, 128), np.float32)
    sgu[:, 0:64] = np.asarray(inputs["L0_ffn_s_gate"], np.float32).reshape(64, 128).T
    sgu[:, 64:128] = np.asarray(inputs["L0_ffn_s_up"], np.float32).reshape(64, 128).T

    com = {
        "w_r": tile_lhsT(np.asarray(inputs["L0_attn_w_r"], np.float32).T, 16, 128),
        "w_k": tile_lhsT(np.asarray(inputs["L0_attn_w_k"], np.float32).T, 16, 128),
        "w_g": tile_lhsT(np.asarray(inputs["L0_attn_w_g"], np.float32).T, 16, 128),
        "w_o": tile_lhsT(np.asarray(inputs["L0_attn_w_o"], np.float32).T, 16, 128),
        "w_v": tile_lhsT(np.asarray(inputs["L0_attn_w_v"], np.float32).T, 4, 512),
        "w_gate": tile_lhsT(np.asarray(inputs["L0_ffn_w_gate"]), 64, 128),
        "w_up": tile_lhsT(np.asarray(inputs["L0_ffn_w_up"]), 64, 128),
        "w_down": tile_lhsT(np.asarray(inputs["L0_ffn_w_down"]), 16, 128),
        "smalls": smalls,
        "sgu": sgu,
        "wkv0": np.ascontiguousarray(np.asarray(inputs["L0_attn_wkvstate"], np.float32)),
        "iota_d": np.ascontiguousarray(np.broadcast_to(np.arange(T, dtype=np.float32), (128, T))),
        "ones_cb": np.ones((128, 1), ml_dtypes.bfloat16),
        "ones_rf": np.ones((1, 128), np.float32),
        "maskS": np.ascontiguousarray(np.triu(np.ones((128, 128), np.float32), 1)),
        "b2c": np.ascontiguousarray(np.kron(np.eye(2, dtype=np.float32), np.ones((64, 1), np.float32))),
    }
    maps = []
    for ci in range(NCORES):
        xs = x[ci * B_LOC:(ci + 1) * B_LOC]
        m = dict(com)
        m["x_fm"] = np.ascontiguousarray(xs.transpose(0, 2, 1)).reshape(B_LOC, NCC, 128, T)
        maps.append(m)
    return maps


_cache = {}


def kernel(**inputs):
    if "nc" not in _cache:
        _cache["nc"] = build_program(debug=DEBUG)
    nc = _cache["nc"]
    maps = _prep_inputs(inputs)
    res = run_bass_kernel_spmd(nc, maps, core_ids=list(range(NCORES)))
    _cache["res"] = res
    outs = []
    for ci in range(NCORES):
        o = res.results[ci]["out_fm"].reshape(B_LOC, C, T)
        outs.append(np.ascontiguousarray(o.transpose(0, 2, 1)))
    return np.concatenate(outs, axis=0)


# version 15
# speedup vs baseline: 1.0693x; 1.0693x over previous
import numpy as np
import ml_dtypes
import concourse.bass as bass
import concourse.mybir as mybir
from concourse.bass_utils import run_bass_kernel_spmd

F32 = mybir.dt.float32
BF16 = mybir.dt.bfloat16
I8 = mybir.dt.int8
AF = mybir.ActivationFunctionType
ALU = mybir.AluOpType

B_LOC = 2
T = 2048
C = 2048
NCC = 16
NTT = 4
NTC = 16
EPS1 = 1e-5
EPSG = 0.00064
NCORES = 8

ENGINES = ("sync", "scalar", "vector", "tensor", "gpsimd")

DEBUG = False
N_BATCH_EMIT = 2
FORCE_DRAIN = False


class Prog:

    def __init__(self):
        self.ops = {e: [] for e in ENGINES}
        self.count = {}
        self.producer = {}
        self.step = {}
        self.hwm = {e: {} for e in ENGINES}

    def chan(self, name, eng, step):
        self.count[name] = 0
        self.producer[name] = eng
        self.step[name] = step
        return name

    def dchan(self, name):
        if name not in self.count:
            self.count[name] = 0
            self.producer[name] = "dma"
            self.step[name] = 16
        return name

    def emit(self, eng, build, waits=(), inc=None, dr=False):
        wl = []
        for ch, v in waits:
            assert v <= self.count[ch], (eng, ch, v, self.count[ch])
            if v <= 0:
                continue
            if self.producer[ch] == eng and self.step[ch] == 1:
                continue
            if self.hwm[eng].get(ch, 0) >= v:
                continue
            self.hwm[eng][ch] = v
            wl.append((ch, v))
        if inc is not None:
            self.count[inc] += self.step[inc]
        self.ops[eng].append((build, wl, inc, dr))

    def val(self, ch):
        return self.count[ch]


class Slot:
    __slots__ = ("free",)

    def __init__(self):
        self.free = []


def build_program(debug=False):
    nc = bass.Bass("TRN2", target_bir_lowering=False)
    dt = {}

    def din(name, shape, dtype):
        dt[name] = nc.dram_tensor(name, list(shape), dtype, kind="ExternalInput")
        return dt[name]

    x_fm = din("x_fm", (B_LOC, NCC, 128, T), F32)
    w_r = din("w_r", (16, 128, 16, 128), F32)
    w_k = din("w_k", (16, 128, 16, 128), F32)
    w_g = din("w_g", (16, 128, 16, 128), F32)
    w_o = din("w_o", (16, 128, 16, 128), F32)
    w_v = din("w_v", (4, 128, 16, 512), F32)
    w_gate = din("w_gate", (64, 128, 16, 128), I8)
    w_up = din("w_up", (64, 128, 16, 128), I8)
    w_down = din("w_down", (16, 128, 64, 128), I8)
    smalls = din("smalls", (128, 176), F32)
    sgu = din("sgu", (128, 128), F32)
    wkv0 = din("wkv0", (32, 64, 64), F32)
    iota_d = din("iota_d", (128, T), F32)
    ones_cb = din("ones_cb", (128, 1), BF16)
    ones_rf = din("ones_rf", (1, 128), F32)
    maskS = din("maskS", (128, 128), F32)
    b2c = din("b2c", (128, 2), F32)

    out_fm = nc.dram_tensor("out_fm", [B_LOC, NCC, 128, T], F32, kind="ExternalOutput")
    dt["out_fm"] = out_fm

    def dscr(name, shape, dtype):
        dt[name] = nc.dram_tensor(name, list(shape), dtype, kind="Internal")
        return dt[name]

    r_scr = dscr("r_scr", (16, 128, T), BF16)
    kw_scr = dscr("kw_scr", (16, 128, T), BF16)
    g_scr = dscr("g_scr", (16, 128, T), BF16)
    v_scr = dscr("v_scr", (16, NTC, 128, 128), BF16)
    yn_scr = dscr("yn_scr", (16, NTC, 128, 128), BF16)
    x1_scr = dscr("x1_scr", (NCC, 128, T), F32)

    dbg = {}
    if debug:
        for nm, shp in [("dbg_ln", (NCC, 128, T)), ("dbg_sT", (128, NTC, 32)),
                        ("dbg_yn", (16, NTC, 128, 128)), ("dbg_x1", (NCC, 128, T)),
                        ("dbg_ln2", (NCC, 128, T)), ("dbg_r", (16, 128, T)), ("dbg_z", (NCC, 128, T)),
                        ("dbg_fuse", (64, 128, 512)),
                        ("dbg_kw", (16, 128, T)), ("dbg_v", (16, NTC, 128, 128)),
                        ("dbg_g", (16, 128, T)), ("dbg_ys", (128, NTC, 128))]:
            dbg[nm] = nc.dram_tensor(nm, list(shp), F32, kind="ExternalOutput")

    ara_off, _ = nc.bump_sbuf(65536)

    def at(name, shape, dtype, off):
        return nc.alloc_sbuf_tensor_at(name, list(shape), dtype, offset=ara_off + off)

    ARA = at("ARA", (128, 32768), BF16, 0)
    XT = [at("XT0", (128, T), F32, 0), at("XT1", (128, T), F32, 8192)]
    XB = [at("XB0", (128, T), BF16, 16384), at("XB1", (128, T), BF16, 20480)]
    XSQ = [at("XSQ0", (128, T), BF16, 24576), at("XSQ1", (128, T), BF16, 28672)]
    ROWS = at("ROWS", (1, 3 * T), F32, 32768)
    RT = [at("RT0", (128, T), BF16, 0), at("RT1", (128, T), BF16, 4096)]
    KW = [at("KW0", (128, T), BF16, 8192), at("KW1", (128, T), BF16, 12288)]
    KC = [at("KC0", (128, NTC, 128), BF16, 16384), at("KC1", (128, NTC, 128), BF16, 20480)]
    VS = [at("VS0", (128, NTC, 128), BF16, 24576), at("VS1", (128, NTC, 128), BF16, 28672)]
    WFOR = at("WFOR", (128, T), BF16, 32768)
    RWB = at("RWB", (128, T), BF16, 36864)
    YS = at("YS", (128, NTC, 128), BF16, 40960)
    YN = at("YN", (128, NTC, 128), BF16, 45056)
    ATC = [at("ATC0", (128, 128), BF16, 49152), at("ATC1", (128, 128), BF16, 49408)]

    LN = nc.alloc_sbuf_tensor("LN", [128, NCC, T + 1], BF16)
    WB = [nc.alloc_sbuf_tensor(f"WB{i}", [128, 8192], BF16) for i in range(2)]
    mub_off, _ = nc.bump_sbuf(16384)

    def at2(name, shape, dtype, off):
        return nc.alloc_sbuf_tensor_at(name, list(shape), dtype, offset=mub_off + off)

    MUB = at2("MUB", (128, 2 * T), F32, 0)
    GBUF = [at2("GB0", (128, T), BF16, 0), at2("GB1", (128, T), BF16, 4096)]
    DTB = nc.alloc_sbuf_tensor("DTB", [128, T], BF16)
    STG = [nc.alloc_sbuf_tensor(f"STG{i}", [128, 512], F32) for i in range(4)]
    SBF = [nc.alloc_sbuf_tensor(f"SBF{i}", [128, 512], BF16) for i in range(6)]
    IOT = nc.alloc_sbuf_tensor("IOT", [128, T], F32)
    SM = nc.alloc_sbuf_tensor("SM", [128, 256], F32)
    SGU = nc.alloc_sbuf_tensor("SGU", [128, 128], F32)
    STT = nc.alloc_sbuf_tensor("STT", [128, NTC * 32], BF16)
    MK = nc.alloc_sbuf_tensor("MK", [128, 128], F32)
    BNS = nc.alloc_sbuf_tensor("BNS", [128, 192], F32)
    GNC = nc.alloc_sbuf_tensor("GNC", [128, 160], F32)
    SSB = nc.alloc_sbuf_tensor("SSB", [128, 64], BF16)
    S0B = nc.alloc_sbuf_tensor("S0B", [128, 64], BF16)
    ONESF = nc.alloc_sbuf_tensor("ONESF", [1, 128], F32)
    ONESB = nc.alloc_sbuf_tensor("ONESB", [128, 1], BF16)
    B2F = nc.alloc_sbuf_tensor("B2F", [128, 2], F32)
    B2U = nc.alloc_sbuf_tensor("B2U", [128, 16, 2], BF16)

    assert nc.sbuf_bytes_remaining >= 0, nc.sbuf_bytes_remaining

    pb = [nc.alloc_psum_tensor(f"pb{i}", [128, 512], F32) for i in range(8)]

    p = Prog()
    pe = p.chan("pe", "tensor", 1)
    dbgc = p.chan("dbgc", "gpsimd", 16)
    dve = p.chan("dve", "vector", 1)
    act = p.chan("act", "scalar", 1)
    ldc = p.dchan("ldc")
    for _n in ("xb0", "xb1", "xt0", "xt1", "wb0", "wb1", "rrd", "se0", "se1",
               "art0", "art1", "akw0", "akw1", "avs0", "avs1", "akc0", "akc1",
               "as0", "syn", "zt", "gb0", "gb1", "sx0", "sx1", "sxs0", "sxs1",
               "so0", "so1"):
        p.dchan(_n)

    s_xt = [Slot(), Slot()]
    s_xb = [Slot(), Slot()]
    s_xsq = [Slot(), Slot()]
    s_wb = [Slot(), Slot()]
    s_ara = Slot()
    s_ln = Slot()
    s_dtb = Slot()
    s_mub = Slot()
    s_stg = [Slot() for _ in range(4)]
    s_sbf = [Slot() for _ in range(6)]
    s_rt = [Slot(), Slot()]
    s_kw = [Slot(), Slot()]
    s_kc = [Slot(), Slot()]
    s_vs = [Slot(), Slot()]
    s_gb = [Slot(), Slot()]
    s_yn_store = [0]
    pbf = [[] for _ in range(8)]

    p.emit("sync", lambda e: e.dma_start(SM[:, 0:176], smalls[:, :]), inc=ldc)
    p.emit("sync", lambda e: e.dma_start(SGU[:, :], sgu[:, :]), inc=ldc)
    p.emit("sync", lambda e: e.dma_start(MK[:, :], maskS[:, :]), inc=ldc)
    p.emit("sync", lambda e: e.dma_start(IOT[:, :], iota_d[:, :]), inc=ldc)
    p.emit("sync", lambda e: e.dma_start(ONESF[:, :], ones_rf[:, :]), inc=ldc)
    p.emit("sync", lambda e: e.dma_start(ONESB[:, :], ones_cb[:, :]), inc=ldc)
    p.emit("sync", lambda e: e.dma_start(B2F[:, :], b2c[:, :]), inc=ldc)
    v_setup = p.val(ldc)
    p.emit("scalar", lambda e: e.activation(SM[:, 176:192], SM[:, 64:80], AF.Exp),
           waits=[(ldc, v_setup)], inc=act, dr=True)
    v_e = p.val(act)
    p.emit("vector", lambda e: e.tensor_scalar_mul(SM[:, 192:208], SM[:, 176:192], -(T - 1.0)),
           waits=[(act, v_e)], inc=dve, dr=True)
    p.emit("vector", lambda e: e.tensor_scalar_mul(SM[:, 208:224], SM[:, 176:192], -1.0), inc=dve, dr=True)
    for fc in range(16):
        p.emit("vector", lambda e, fc=fc: e.tensor_scalar_mul(
            B2U[:, fc, :], B2F[:, :], SM[:, 144 + fc:145 + fc]),
            waits=[(ldc, v_setup)], inc=dve, dr=True)
    p.emit("scalar", lambda e: e.activation(SM[:, 226:227], SM[:, 64:65], AF.Silu),
           waits=[(ldc, v_setup)], inc=act, dr=True)
    p.emit("vector", lambda e: e.memset(SM[:, 224:225], EPS1), inc=dve, dr=True)
    p.emit("vector", lambda e: e.memset(SM[:, 225:226], EPSG), inc=dve, dr=True)
    p.emit("vector", lambda e: e.tensor_copy(
        LN[:, :, 0:1], SM[:, 128:144].rearrange("p (c o) -> p c o", o=1)),
        waits=[(ldc, v_setup)], inc=dve, dr=True)
    v_smalls = p.val(dve)

    def emit_ln(src_ap, eps_ap):
        guard0 = list(s_ara.free)
        pe_tail = [0, 0]
        for cc in range(NCC):
            xb, xq = XB[cc % 2], XSQ[cc % 2]
            slb, slq = s_xb[cc % 2], s_xsq[cc % 2]
            ch = p.dchan(f"xb{cc % 2}")
            p.emit("gpsimd", lambda e, cc=cc, xb=xb: e.dma_start(xb[:, :], src_ap(cc)),
                   waits=(guard0 if cc < 2 else []) + slb.free, inc=ch)
            v_ld = p.val(ch)
            p.emit("scalar", lambda e, xb=xb, xq=xq: e.activation(xq[:, :], xb[:, :], AF.Square),
                   waits=[(ch, v_ld)] + slq.free, inc=act, dr=True)
            v_sq = p.val(act)
            for tt in range(NTT):
                p.emit("tensor", lambda e, xb=xb, tt=tt, cc=cc: e.matmul(
                    pb[tt][0:1, :], ONESB[:, :], xb[:, tt * 512:(tt + 1) * 512],
                    start=(cc == 0), stop=(cc == NCC - 1), skip_group_check=True),
                    waits=[(ch, v_ld), (ldc, v_setup)] + (pbf[tt] if cc == 0 else []), inc=None)
                p.emit("tensor", lambda e, xq=xq, tt=tt, cc=cc: e.matmul(
                    pb[4 + tt][0:1, :], ONESB[:, :], xq[:, tt * 512:(tt + 1) * 512],
                    start=(cc == 0), stop=(cc == NCC - 1), skip_group_check=True),
                    waits=[(act, v_sq)] + (pbf[4 + tt] if cc == 0 else []),
                    inc=(pe if tt == NTT - 1 else None))
            slb.free = [(pe, p.val(pe))]
            slq.free = [(pe, p.val(pe))]
        v_stats = p.val(pe)
        for tt in range(NTT):
            p.emit("vector", lambda e, tt=tt: e.tensor_copy(ROWS[0:1, tt * 512:(tt + 1) * 512], pb[tt][0:1, :]),
                   waits=[(pe, v_stats)], inc=dve, dr=(tt == NTT - 1))
            p.emit("vector", lambda e, tt=tt: e.tensor_copy(ROWS[0:1, T + tt * 512:T + (tt + 1) * 512], pb[4 + tt][0:1, :]),
                   inc=dve, dr=(tt == NTT - 1))
        p.emit("vector", lambda e: e.tensor_scalar_mul(ROWS[0:1, 0:T], ROWS[0:1, 0:T], 1.0 / C), inc=dve, dr=True)
        p.emit("vector", lambda e: e.tensor_scalar_mul(ROWS[0:1, T:2 * T], ROWS[0:1, T:2 * T], 1.0 / C), inc=dve, dr=True)
        p.emit("vector", lambda e: e.tensor_tensor(ROWS[0:1, 2 * T:3 * T], ROWS[0:1, 0:T], ROWS[0:1, 0:T], ALU.mult), inc=dve, dr=True)
        p.emit("vector", lambda e: e.tensor_tensor(ROWS[0:1, T:2 * T], ROWS[0:1, T:2 * T], ROWS[0:1, 2 * T:3 * T], ALU.subtract), inc=dve, dr=True)
        p.emit("vector", lambda e: e.tensor_scalar_max(ROWS[0:1, T:2 * T], ROWS[0:1, T:2 * T], 0.0), inc=dve, dr=True)
        v_var = p.val(dve)
        p.emit("scalar", lambda e: e.activation(ROWS[0:1, T:2 * T], ROWS[0:1, T:2 * T], AF.Ln, bias=eps_ap),
               waits=[(dve, v_var)], inc=act, dr=True)
        p.emit("scalar", lambda e: e.activation(ROWS[0:1, T:2 * T], ROWS[0:1, T:2 * T], AF.Exp, scale=-0.5), inc=act, dr=True)
        v_rstd = p.val(act)
        p.emit("vector", lambda e: e.tensor_scalar_mul(ROWS[0:1, 0:T], ROWS[0:1, 0:T], -1.0),
               waits=[(act, v_rstd)], inc=dve, dr=True)
        v_rows = p.val(dve)
        for tt in range(NTT):
            p.emit("tensor", lambda e, tt=tt: e.matmul(
                pb[tt][:, :], ONESF[:, :], ROWS[0:1, tt * 512:(tt + 1) * 512], start=True, stop=True),
                waits=[(dve, v_rows)], inc=None)
            p.emit("tensor", lambda e, tt=tt: e.matmul(
                pb[4 + tt][:, :], ONESF[:, :], ROWS[0:1, T + tt * 512:T + (tt + 1) * 512], start=True, stop=True),
                waits=[(act, v_rstd)], inc=(pe if tt == NTT - 1 else None))
        v_bc = p.val(pe)
        for tt in range(NTT):
            p.emit("vector", lambda e, tt=tt: e.tensor_copy(MUB[:, tt * 512:(tt + 1) * 512], pb[tt][:, :]),
                   waits=[(pe, v_bc)] + (s_mub.free if tt == 0 else []), inc=dve)
            p.emit("vector", lambda e, tt=tt: e.tensor_copy(MUB[:, T + tt * 512:T + (tt + 1) * 512], pb[4 + tt][:, :]),
                   inc=dve, dr=(tt == NTT - 1))
        v_mub = p.val(dve)
        for i in range(8):
            pbf[i] = [(dve, v_mub)]
        for cc in range(NCC):
            xt = XT[cc % 2]
            sl = s_xt[cc % 2]
            ch = p.dchan(f"xt{cc % 2}")
            p.emit("sync", lambda e, cc=cc, xt=xt: e.dma_start(xt[:, :], src_ap(cc)),
                   waits=sl.free + (guard0 if cc < 2 else []), inc=ch)
            v_ld = p.val(ch)
            p.emit("vector", lambda e, xt=xt: e.tensor_tensor(xt[:, :], xt[:, :], MUB[:, 0:T], ALU.add),
                   waits=[(ch, v_ld), (dve, v_mub)] + (s_ln.free if cc == 0 else []), inc=dve, dr=True)
            p.emit("vector", lambda e, xt=xt, cc=cc: e.tensor_tensor(
                LN[:, cc, 1:T + 1], xt[:, :], MUB[:, T:2 * T], ALU.mult), inc=dve)
            sl.free = [(dve, p.val(dve))]
        s_mub.free = [(dve, p.val(dve))]
        return p.val(dve)

    for b in range(N_BATCH_EMIT):
        v_lnd = emit_ln(lambda cc, b=b: x_fm[b, cc, :, :], SM[0:1, 224:225])
        v_lnd = p.val(dve)
        if debug and b == 0:
            for cc in range(NCC):
                p.emit("gpsimd", lambda e, cc=cc: e.dma_start(dbg["dbg_ln"][cc, :, :], LN[:, cc, 1:T + 1]),
                       waits=[(dve, v_lnd)], inc=dbgc)

        def emit_mix(a_off):
            g0 = list(s_ara.free) + list(s_dtb.free)
            for cc in range(NCC):
                p.emit("vector", lambda e, cc=cc: e.tensor_tensor(
                    DTB[:, :], LN[:, cc, 1:T + 1], LN[:, cc, 0:T], ALU.subtract),
                    waits=(g0 if cc == 0 else []), inc=dve, dr=True)
                p.emit("vector", lambda e, cc=cc, ao=a_off: e.scalar_tensor_tensor(
                    ARA[:, cc * 2048:(cc + 1) * 2048], DTB[:, :], SM[:, ao + cc:ao + cc + 1],
                    LN[:, cc, 0:T], ALU.mult, ALU.add), inc=dve, dr=(cc == NCC - 1))
            s_dtb.free = [(dve, p.val(dve))]
            return p.val(dve)

        def fm_proj(wdram, evac):
            v_mix = p.val(dve)
            for o in range(16):
                wb_ = WB[o % 2]
                slw = s_wb[o % 2]
                chw = p.dchan(f"wb{o % 2}")
                p.emit("gpsimd", lambda e, o=o, wb_=wb_, w=wdram: e.dma_start(
                    wb_[:, 0:2048], w[o].rearrange("p cc o -> p (cc o)")),
                    waits=slw.free, inc=chw)
                v_w = p.val(chw)
                for tt in range(NTT):
                    bi = (o * NTT + tt) % 4
                    bank = pb[bi]
                    for cc in range(NCC):
                        p.emit("tensor", lambda e, cc=cc, tt=tt, wb_=wb_, bank=bank: e.matmul(
                            bank[:, :], wb_[:, cc * 128:(cc + 1) * 128],
                            ARA[:, cc * 2048 + tt * 512: cc * 2048 + (tt + 1) * 512],
                            start=(cc == 0), stop=(cc == NCC - 1), skip_group_check=True),
                            waits=[(chw, v_w), (dve, v_mix)] + (pbf[bi] if cc == 0 else []),
                            inc=(pe if cc == NCC - 1 else None))
                    evac(o, tt, bank, p.val(pe), bi)
                slw.free = [(pe, p.val(pe))]
            s_ara.free = [(pe, p.val(pe))]

        def evac_r(o, tt, bank, v_pe, bi):
            st = SBF[tt % 2]
            sl = s_sbf[tt % 2]
            p.emit("vector", lambda e, bank=bank, st=st: e.tensor_copy(st[:, :], bank[:, :]),
                   waits=[(pe, v_pe)] + sl.free, inc=dve, dr=True)
            pbf[bi] = [(dve, p.val(dve))]
            v_c = p.val(dve)
            che = p.dchan(f"se{tt % 2}")
            p.emit("scalar", lambda e, o=o, tt=tt, st=st: e.dma_start(
                r_scr[o, :, tt * 512:(tt + 1) * 512], st[:, :]), waits=[(dve, v_c)], inc=che)
            sl.free = [(che, p.val(che))]

        emit_mix(32)
        fm_proj(w_r, evac_r)
        v_r_stored = [("se0", p.val("se0")), ("se1", p.val("se1"))]

        def evac_k(o, tt, bank, v_pe, bi):
            wbs = STG[2]
            slb = s_stg[2]
            p.emit("scalar", lambda e, o=o, tt=tt: e.activation(
                STG[2][:, :], IOT[:, tt * 512:(tt + 1) * 512], AF.Exp,
                scale=SM[:, 176 + o:177 + o], bias=SM[:, 192 + o:193 + o]),
                waits=slb.free + [(dve, v_smalls)], inc=act, dr=True)
            v_wb = p.val(act)
            rt_ = SBF[4]
            slr = s_sbf[4]
            chr_ = p.dchan("rrd")
            p.emit("sync", lambda e, o=o, tt=tt: e.dma_start(
                SBF[4][:, :], r_scr[o, :, tt * 512:(tt + 1) * 512]),
                waits=slr.free + v_r_stored, inc=chr_)
            v_rt = p.val(chr_)
            st = SBF[tt % 2]
            sl = s_sbf[tt % 2]
            p.emit("vector", lambda e, bank=bank, st=st: e.tensor_tensor(
                st[:, :], bank[:, :], STG[2][:, :], ALU.mult),
                waits=[(pe, v_pe), (act, v_wb)] + sl.free, inc=dve, dr=True)
            v_kw = p.val(dve)
            slb.free = [(dve, v_kw)]
            che = p.dchan(f"se{tt % 2}")
            p.emit("scalar", lambda e, o=o, tt=tt, st=st: e.dma_start(
                kw_scr[o, :, tt * 512:(tt + 1) * 512], st[:, :]), waits=[(dve, v_kw)], inc=che)
            sl.free = [(che, p.val(che))]
            pr = SBF[5]
            slp = s_sbf[5]
            p.emit("vector", lambda e, bank=bank, pr=pr: e.tensor_tensor(
                pr[:, :], bank[:, :], SBF[4][:, :], ALU.mult),
                waits=[(chr_, v_rt)] + slp.free, inc=dve, dr=True)
            v_pr = p.val(dve)
            slr.free = [(dve, v_pr)]
            pbf[bi] = [(dve, v_pr)]
            for ts in range(4):
                col = (tt * 4 + ts) * 2
                p.emit("tensor", lambda e, ts=ts, col=col, pr=pr, o=o: e.matmul(
                    pb[7][:, col:col + 2], pr[:, ts * 128:(ts + 1) * 128], B2U[:, o, :],
                    start=True, stop=True, skip_group_check=True),
                    waits=[(dve, v_pr), (dve, v_smalls)] + (pbf[7] if (tt == 0 and ts == 0) else []),
                    inc=(pe if ts == 3 else None))
            slp.free = [(pe, p.val(pe))]
            if tt == NTT - 1:
                v_s = p.val(pe)
                p.emit("vector", lambda e, o=o: e.tensor_copy(
                    STT[:, :].rearrange("p (tc h) -> p tc h", h=32)[:, :, 2 * o:2 * o + 2],
                    pb[7][:, 0:32].rearrange("p (tc h) -> p tc h", h=2)),
                    waits=[(pe, v_s)], inc=dve, dr=True)
                pbf[7] = [(dve, p.val(dve))]

        emit_mix(0)
        fm_proj(w_k, evac_k)

        def evac_g(o, tt, bank, v_pe, bi):
            st = SBF[tt % 2]
            sl = s_sbf[tt % 2]
            p.emit("scalar", lambda e, bank=bank, st=st: e.activation(st[:, :], bank[:, :], AF.Silu),
                   waits=[(pe, v_pe)] + sl.free, inc=act, dr=True)
            pbf[bi] = [(act, p.val(act))]
            v_c = p.val(act)
            che = p.dchan(f"se{tt % 2}")
            p.emit("scalar", lambda e, o=o, tt=tt, st=st: e.dma_start(
                g_scr[o, :, tt * 512:(tt + 1) * 512], st[:, :]), waits=[(act, v_c)], inc=che)
            sl.free = [(che, p.val(che))]

        emit_mix(48)
        fm_proj(w_g, evac_g)

        emit_mix(16)
        v_mixv = p.val(dve)
        for oc in range(4):
            wb_ = WB[oc % 2]
            slw = s_wb[oc % 2]
            chw = p.dchan(f"wb{oc % 2}")
            p.emit("gpsimd", lambda e, oc=oc, wb_=wb_: e.dma_start(
                wb_[:, :], w_v[oc].rearrange("p cc o -> p (cc o)")),
                waits=slw.free, inc=chw)
            v_w = p.val(chw)
            for tc in range(NTC):
                bi = (oc * NTC + tc) % 4
                bank = pb[bi]
                for cc in range(NCC):
                    p.emit("tensor", lambda e, cc=cc, tc=tc, wb_=wb_, bank=bank: e.matmul(
                        bank[:, :], ARA[:, cc * 2048 + tc * 128: cc * 2048 + (tc + 1) * 128],
                        wb_[:, cc * 512:(cc + 1) * 512],
                        start=(cc == 0), stop=(cc == NCC - 1), skip_group_check=True),
                        waits=[(chw, v_w), (dve, v_mixv)] + (pbf[bi] if cc == 0 else []),
                        inc=(pe if cc == NCC - 1 else None))
                v_pe = p.val(pe)
                st = SBF[tc % 2]
                sl = s_sbf[tc % 2]
                p.emit("vector", lambda e, bank=bank, st=st: e.tensor_copy(st[:, :], bank[:, :]),
                       waits=[(pe, v_pe)] + sl.free, inc=dve, dr=True)
                pbf[bi] = [(dve, p.val(dve))]
                v_c = p.val(dve)
                che = p.dchan(f"se{tc % 2}")
                for j in range(4):
                    p.emit("scalar", lambda e, oc=oc, tc=tc, j=j, st=st: e.dma_start(
                        v_scr[oc * 4 + j, tc, :, :], st[:, j * 128:(j + 1) * 128]),
                        waits=[(dve, v_c)], inc=che)
                sl.free = [(che, p.val(che))]
            slw.free = [(pe, p.val(pe))]
        s_ara.free = [(pe, p.val(pe))]
        v_proj_done = [("se0", p.val("se0")), ("se1", p.val("se1"))]
        if debug and b == 0:
            p.emit("gpsimd", lambda e: e.dma_start(dbg["dbg_r"][:, :, :], r_scr[:, :, :]),
                   waits=v_proj_done, inc=dbgc)
            p.emit("gpsimd", lambda e: e.dma_start(dbg["dbg_kw"][:, :, :], kw_scr[:, :, :]),
                   waits=v_proj_done, inc=dbgc)
            p.emit("gpsimd", lambda e: e.dma_start(dbg["dbg_v"][:, :, :, :], v_scr[:, :, :, :]),
                   waits=v_proj_done, inc=dbgc)
            p.emit("gpsimd", lambda e: e.dma_start(dbg["dbg_g"][:, :, :], g_scr[:, :, :]),
                   waits=v_proj_done, inc=dbgc)

        atc_free = [[], []]
        s_copy = [0, 0]
        y_evac = [0, 0]
        s_s0 = Slot()
        s_wfor = Slot()
        s_rw = Slot()
        s_ssb = Slot()
        for sl in (s_rt + s_kw + s_kc + s_vs + s_gb):
            sl.free = list(s_ara.free)
        v_ys_guard = 0
        for fc in range(16):
            rt_, kw_, kc_, vs_ = RT[fc % 2], KW[fc % 2], KC[fc % 2], VS[fc % 2]
            slr, slk, slc, slv = s_rt[fc % 2], s_kw[fc % 2], s_kc[fc % 2], s_vs[fc % 2]
            chrt = p.dchan(f"art{fc % 2}")
            chkw = p.dchan(f"akw{fc % 2}")
            chvs = p.dchan(f"avs{fc % 2}")
            chkc = p.dchan(f"akc{fc % 2}")
            chs0 = p.dchan("as0")
            p.emit("sync", lambda e, fc=fc, d=rt_: e.dma_start(d[:, :], r_scr[fc, :, :]),
                   waits=v_proj_done + slr.free, inc=chrt)
            v_r = p.val(chrt)
            p.emit("sync", lambda e, fc=fc, d=kw_: e.dma_start(d[:, :], kw_scr[fc, :, :]),
                   waits=slk.free, inc=chkw)
            v_kw = p.val(chkw)
            p.emit("sync", lambda e, fc=fc, d=vs_: e.dma_start(
                d[:, :, :], v_scr[fc].rearrange("tc p j -> p tc j")),
                waits=slv.free, inc=chvs)
            v_v = p.val(chvs)
            for tc in range(NTC):
                p.emit("scalar", lambda e, fc=fc, tc=tc, d=kc_: e.dma_start(
                    d[:, tc, :], kw_scr[fc, :, tc * 128:(tc + 1) * 128], transpose=True),
                    waits=((v_proj_done + slc.free) if tc == 0 else []), inc=chkc)
            v_kc = p.val(chkc)
            p.emit("gpsimd", lambda e, fc=fc: e.dma_start(S0B[0:64, :], wkv0[2 * fc, :, :]),
                   waits=s_s0.free, inc=chs0)
            p.emit("gpsimd", lambda e, fc=fc: e.dma_start(S0B[64:128, :], wkv0[2 * fc + 1, :, :]), inc=chs0)
            v_s0 = p.val(chs0)
            p.emit("scalar", lambda e, fc=fc: e.activation(
                WFOR[:, :], IOT[:, :], AF.Exp, scale=SM[:, 208 + fc:209 + fc]),
                waits=s_wfor.free + [(dve, v_smalls)], inc=act, dr=True)
            v_wf = p.val(act)
            p.emit("vector", lambda e, d=rt_: e.tensor_tensor(RWB[:, :], d[:, :], WFOR[:, :], ALU.mult),
                   waits=[(chrt, v_r), (act, v_wf)] + s_rw.free, inc=dve, dr=True)
            v_rw = p.val(dve)
            s_wfor.free = [(dve, v_rw)]
            p.emit("vector", lambda e: e.memset(SSB[:, :], 0.0), waits=s_ssb.free, inc=dve, dr=True)
            v_sz = p.val(dve)

            for tc in range(NTC):
                for hh in range(2):
                    hs = slice(64 * hh, 64 * hh + 64)
                    bank_a = pb[hh]
                    p.emit("tensor", lambda e, hh=hh, tc=tc, hs=hs, bank=bank_a, d=kw_, r=rt_: e.matmul(
                        bank[:, 0:128], d[hs, tc * 128:(tc + 1) * 128], r[hs, tc * 128:(tc + 1) * 128],
                        start=True, stop=True, skip_group_check=True),
                        waits=[(chkw, v_kw), (chrt, v_r)] + atc_free[hh] +
                              (pbf[hh] if tc == 0 else []), inc=pe)
                    v_att = p.val(pe)
                    atc = ATC[hh]
                    p.emit("vector", lambda e, bank=bank_a, atc=atc: e.tensor_tensor(
                        atc[:, :], bank[:, 0:128], MK[:, :], ALU.mult),
                        waits=[(pe, v_att)], inc=dve, dr=True)
                    v_atc = p.val(dve)
                    atc_free[hh] = [(pe, 0)]
                    bank_o = pb[2 + hh]
                    oreg = bank_o[:, 0:64]
                    p.emit("tensor", lambda e, atc=atc, tc=tc, hh=hh, v=vs_, oreg=oreg: e.matmul(
                        oreg, atc[:, :], v[:, tc, hs_c(hh)], start=True, stop=False,
                        skip_group_check=True),
                        waits=[(dve, v_atc), (chvs, v_v)] +
                              ([(dve, y_evac[hh])] if y_evac[hh] else []) +
                              (pbf[2 + hh] if tc == 0 else []), inc=None)
                    if tc > 0:
                        p.emit("tensor", lambda e, r=rt_, hs=hs, tc=tc, oreg=oreg, hh=hh: e.matmul(
                            oreg, r[hs, tc * 128:(tc + 1) * 128], SSB[hs, :],
                            start=False, stop=False, tile_position=(64 * hh, 0),
                            skip_group_check=True),
                            waits=[(act, s_copy[hh])], inc=None)
                    p.emit("tensor", lambda e, hs=hs, tc=tc, oreg=oreg, hh=hh: e.matmul(
                        oreg, RWB[hs, tc * 128:(tc + 1) * 128], S0B[hs, :],
                        start=False, stop=True, tile_position=(64 * hh, 0),
                        skip_group_check=True),
                        waits=[(dve, v_rw), (chs0, v_s0)], inc=pe)
                    v_out = p.val(pe)
                    atc_free[hh] = [(pe, v_out)]
                    bank_s = pb[6 + hh]
                    p.emit("tensor", lambda e, d=kc_, tc=tc, hh=hh, v=vs_, bank=bank_s, hs=hs: e.matmul(
                        bank[hs, 0:64], d[:, tc, hs_c(hh)], v[:, tc, hs_c(hh)],
                        start=(tc == 0), stop=(tc == NTC - 1), tile_position=(0, 64 * hh),
                        skip_group_check=True),
                        waits=[(chkc, v_kc)] +
                              ([(dve, v_sz)] if tc == 0 else [(act, s_copy[hh])]) +
                              (pbf[6 + hh] if tc == 0 else []), inc=pe)
                    v_su = p.val(pe)
                    if tc < NTC - 1:
                        p.emit("scalar", lambda e, bank=bank_s, hs=hs: e.copy(SSB[hs, :], bank[hs, 0:64]),
                               waits=[(pe, v_su)], inc=act, dr=True)
                        s_copy[hh] = p.val(act)
                    p.emit("vector", lambda e, v=vs_, tc=tc, hh=hh, fc=fc, oreg=oreg: e.scalar_tensor_tensor(
                        YS[:, tc, hs_c(hh)], v[:, tc, hs_c(hh)],
                        STT[:, :].rearrange("p (tc h) -> p tc h", h=32)[:, tc, 2 * fc + hh:2 * fc + hh + 1],
                        oreg, ALU.mult, ALU.add),
                        waits=[(pe, v_out)] +
                              ([("syn", s_yn_store[0])] if (tc == 0 and hh == 0) else []), inc=dve,
                        dr=(tc == NTC - 1 and hh == 1))
                    y_evac[hh] = p.val(dve)
            v_scan_pe = p.val(pe)
            v_scan = p.val(dve)
            s_ssb.free = [(pe, v_scan_pe)]
            s_rw.free = [(pe, v_scan_pe)]
            s_s0.free = [(pe, v_scan_pe)]
            slr.free = [(pe, v_scan_pe), (dve, v_rw)]
            slk.free = [(pe, v_scan_pe)]
            slc.free = [(pe, v_scan_pe)]
            slv.free = [(dve, v_scan), (pe, v_scan_pe)]
            pbf[0] = [(dve, v_scan)]
            pbf[1] = [(dve, v_scan)]
            pbf[2] = [(dve, v_scan)]
            pbf[3] = [(dve, v_scan)]
            pbf[6] = [(act, s_copy[0]), (dve, v_scan)]
            pbf[7] = [(act, s_copy[1]), (dve, v_scan)]
            s_copy = [0, 0]
            y_evac = [0, 0]

            for tc in range(NTC):
                for hh in range(2):
                    g = tc * 2 + hh
                    p.emit("vector", lambda e, tc=tc, hh=hh, g=g: e.bn_stats(
                        BNS[:, g * 6:(g + 1) * 6], YS[:, tc, hs_c(hh)]), inc=dve,
                        dr=(g == 31))
            bm1 = BNS[:, 1:192:6]
            bm2 = BNS[:, 4:192:6]
            bq1 = BNS[:, 2:192:6]
            bq2 = BNS[:, 5:192:6]
            p.emit("vector", lambda e: e.tensor_tensor(GNC[:, 0:32], bm1, bm2, ALU.add), inc=dve)
            p.emit("vector", lambda e: e.tensor_tensor(GNC[:, 128:160], bm1, bm2, ALU.subtract), inc=dve)
            p.emit("vector", lambda e: e.tensor_scalar_mul(GNC[:, 0:32], GNC[:, 0:32], 0.5), inc=dve)
            p.emit("vector", lambda e: e.tensor_scalar_mul(GNC[:, 128:160], GNC[:, 128:160], 0.5), inc=dve)
            p.emit("vector", lambda e: e.tensor_tensor(GNC[:, 32:64], bq1, bq2, ALU.add), inc=dve)
            p.emit("vector", lambda e: e.tensor_tensor(GNC[:, 128:160], GNC[:, 128:160], GNC[:, 128:160], ALU.mult), inc=dve)
            p.emit("vector", lambda e: e.tensor_scalar_mul(GNC[:, 32:64], GNC[:, 32:64], 1.0 / 64), inc=dve, dr=True)
            p.emit("vector", lambda e: e.tensor_tensor(GNC[:, 32:64], GNC[:, 32:64], GNC[:, 128:160], ALU.add), inc=dve, dr=True)
            p.emit("vector", lambda e: e.tensor_scalar_max(GNC[:, 32:64], GNC[:, 32:64], 0.0), inc=dve, dr=True)
            v_gn = p.val(dve)
            p.emit("scalar", lambda e: e.activation(GNC[:, 64:96], GNC[:, 32:64], AF.Ln, bias=SM[:, 225:226]),
                   waits=[(dve, v_gn)], inc=act, dr=True)
            p.emit("scalar", lambda e: e.activation(GNC[:, 64:96], GNC[:, 64:96], AF.Exp, scale=-0.5), inc=act, dr=True)
            v_rstd = p.val(act)
            p.emit("vector", lambda e: e.tensor_tensor(GNC[:, 96:128], GNC[:, 0:32], GNC[:, 64:96], ALU.mult),
                   waits=[(act, v_rstd)], inc=dve, dr=True)
            p.emit("vector", lambda e: e.tensor_scalar_mul(GNC[:, 96:128], GNC[:, 96:128], -1.0), inc=dve, dr=True)
            v_gnb = p.val(dve)
            for tc in range(NTC):
                for hh in range(2):
                    g = tc * 2 + hh
                    p.emit("vector", lambda e, tc=tc, hh=hh, g=g: e.tensor_scalar(
                        YN[:, tc, hs_c(hh)], YS[:, tc, hs_c(hh)],
                        GNC[:, 64 + g:65 + g], GNC[:, 96 + g:97 + g], ALU.mult, ALU.add),
                        waits=([("syn", s_yn_store[0])] if (tc == 0 and hh == 0) else []), inc=dve,
                        dr=(tc == NTC - 1 and hh == 1))
            v_yn = p.val(dve)
            p.emit("scalar", lambda e, fc=fc: e.dma_start(
                yn_scr[fc].rearrange("tc p j -> p tc j"), YN[:, :, :]),
                waits=[(dve, v_yn)], inc="syn")
            s_yn_store[0] = p.val("syn")
            if debug and b == 0 and fc == 0:
                p.emit("gpsimd", lambda e: e.dma_start(dbg["dbg_ys"][:, :, :], YS[:, :, :]),
                       waits=[(dve, v_yn)], inc=dbgc)
                pass
        s_ara.free = [(pe, p.val(pe)), (dve, p.val(dve))]
        v_A_done = p.val("syn")
        if debug and b == 0:
            p.emit("gpsimd", lambda e: e.dma_start(dbg["dbg_yn"][:, :, :, :], yn_scr[:, :, :, :]),
                   waits=[("syn", v_A_done)], inc=dbgc)
            p.emit("gpsimd", lambda e: e.dma_start(
                dbg["dbg_sT"][:, :, :], STT[:, :].rearrange("p (tc h) -> p tc h", h=32)),
                waits=[("syn", v_A_done)], inc=dbgc)

        g0 = list(s_ara.free)
        for cc in range(NCC):
            p.emit("scalar", lambda e, cc=cc: e.dma_start(
                ARA[:, cc * 2048:(cc + 1) * 2048],
                yn_scr[cc].rearrange("tc p j -> (tc p) j"), transpose=True),
                waits=([("syn", v_A_done)] + g0) if cc == 0 else [], inc="zt")
        v_z = p.val("zt")
        for cc in range(NCC):
            p.emit("vector", lambda e, cc=cc: e.tensor_scalar(
                ARA[:, cc * 2048:(cc + 1) * 2048], ARA[:, cc * 2048:(cc + 1) * 2048],
                SM[:, 80 + cc:81 + cc], SM[:, 96 + cc:97 + cc], ALU.mult, ALU.add),
                waits=[("zt", v_z)], inc=dve, dr=True)
            gb = GBUF[cc % 2]
            slg = s_gb[cc % 2]
            chg = p.dchan(f"gb{cc % 2}")
            p.emit("sync", lambda e, cc=cc, d=gb: e.dma_start(d[:, :], g_scr[cc, :, :]),
                   waits=slg.free + (s_mub.free if cc < 2 else []), inc=chg)
            v_g = p.val(chg)
            p.emit("vector", lambda e, cc=cc, d=gb: e.tensor_tensor(
                ARA[:, cc * 2048:(cc + 1) * 2048], ARA[:, cc * 2048:(cc + 1) * 2048],
                d[:, :], ALU.mult),
                waits=[(chg, v_g)], inc=dve, dr=(cc == NCC - 1))
            slg.free = [(dve, p.val(dve))]
        v_zdone = p.val(dve)
        if debug and b == 0:
            for cc_ in range(NCC):
                p.emit("gpsimd", lambda e, cc_=cc_: e.dma_start(
                    dbg["dbg_z"][cc_, :, :], ARA[:, cc_ * 2048:(cc_ + 1) * 2048]),
                    waits=[(dve, v_zdone)], inc=dbgc)
        for o in range(16):
            wb_ = WB[o % 2]
            slw = s_wb[o % 2]
            chw = p.dchan(f"wb{o % 2}")
            p.emit("gpsimd", lambda e, o=o, wb_=wb_: e.dma_start(
                wb_[:, 0:2048], w_o[o].rearrange("p cc o -> p (cc o)")),
                waits=slw.free, inc=chw)
            v_w = p.val(chw)
            for tt in range(NTT):
                bi = (o * NTT + tt) % 4
                bank = pb[bi]
                for cc in range(NCC):
                    p.emit("tensor", lambda e, cc=cc, tt=tt, wb_=wb_, bank=bank: e.matmul(
                        bank[:, :], wb_[:, cc * 128:(cc + 1) * 128],
                        ARA[:, cc * 2048 + tt * 512:cc * 2048 + (tt + 1) * 512],
                        start=(cc == 0), stop=(cc == NCC - 1), skip_group_check=True),
                        waits=[(chw, v_w), (dve, v_zdone)] + (pbf[bi] if cc == 0 else []),
                        inc=(pe if cc == NCC - 1 else None))
                v_pe = p.val(pe)
                xt = STG[tt % 2]
                slx = s_stg[tt % 2]
                chx = p.dchan(f"sx{tt % 2}")
                p.emit("sync", lambda e, b=b, o=o, tt=tt, xt=xt: e.dma_start(
                    xt[:, :], x_fm[b, o, :, tt * 512:(tt + 1) * 512]),
                    waits=slx.free, inc=chx)
                v_x = p.val(chx)
                x1s = STG[2 + (tt % 2)]
                slx1 = s_stg[2 + (tt % 2)]
                p.emit("vector", lambda e, bank=bank, xt=xt, x1s=x1s: e.tensor_tensor(
                    x1s[:, :], bank[:, :], xt[:, :], ALU.add),
                    waits=[(pe, v_pe), (chx, v_x)] + slx1.free, inc=dve, dr=True)
                v_x1 = p.val(dve)
                pbf[bi] = [(dve, v_x1)]
                slx.free = [(dve, v_x1)]
                chxs = p.dchan(f"sxs{tt % 2}")
                p.emit("scalar", lambda e, o=o, tt=tt, x1s=x1s: e.dma_start(
                    x1_scr[o, :, tt * 512:(tt + 1) * 512], x1s[:, :]),
                    waits=[(dve, v_x1)], inc=chxs)
                slx1.free = [(chxs, p.val(chxs))]
            slw.free = [(pe, p.val(pe))]
        s_ara.free = [(pe, p.val(pe))]
        s_mub.free = [(dve, p.val(dve))]
        v_x1_done = [("sxs0", p.val("sxs0")), ("sxs1", p.val("sxs1"))]
        if debug and b == 0:
            p.emit("gpsimd", lambda e: e.dma_start(dbg["dbg_x1"][:, :, :], x1_scr[:, :, :]),
                   waits=v_x1_done, inc=dbgc)

        s_ln.free = [(pe, p.val(pe))]
        s_ara.free = s_ara.free + v_x1_done
        emit_ln(lambda cc: x1_scr[cc, :, :], SM[0:1, 224:225])
        v_ln2 = p.val(dve)
        if debug and b == 0:
            for cc in range(NCC):
                p.emit("gpsimd", lambda e, cc=cc: e.dma_start(dbg["dbg_ln2"][cc, :, :], LN[:, cc, 1:T + 1]),
                       waits=[(dve, v_ln2)], inc=dbgc)

        pbf2 = pbf
        for tt in range(NTT):
            for oc in range(64):
                par = oc % 2
                gv = WB[0][:, par * 2048:(par + 1) * 2048]
                uv = WB[1][:, par * 2048:(par + 1) * 2048]
                slw = s_wb[par]
                chw = p.dchan(f"wb{par}")
                p.emit("gpsimd", lambda e, oc=oc, gv=gv: e.dma_start(
                    gv, w_gate[oc].rearrange("p cc o -> p (cc o)")),
                    waits=slw.free, inc=chw)
                p.emit("gpsimd", lambda e, oc=oc, uv=uv: e.dma_start(
                    uv, w_up[oc].rearrange("p cc o -> p (cc o)")), inc=chw)
                v_w = p.val(chw)
                bg = pb[par * 2]
                bu = pb[par * 2 + 1]
                for cc in range(NCC):
                    p.emit("tensor", lambda e, cc=cc, tt=tt, gv=gv, bg=bg: e.matmul(
                        bg[:, :], gv[:, cc * 128:(cc + 1) * 128],
                        LN[:, cc, 1 + tt * 512:1 + (tt + 1) * 512],
                        start=(cc == 0), stop=(cc == NCC - 1), skip_group_check=True),
                        waits=[(chw, v_w), (dve, v_ln2)] + (pbf[par * 2] if cc == 0 else []),
                        inc=(pe if cc == NCC - 1 else None))
                    p.emit("tensor", lambda e, cc=cc, tt=tt, uv=uv, bu=bu: e.matmul(
                        bu[:, :], uv[:, cc * 128:(cc + 1) * 128],
                        LN[:, cc, 1 + tt * 512:1 + (tt + 1) * 512],
                        start=(cc == 0), stop=(cc == NCC - 1), skip_group_check=True),
                        waits=[(chw, v_w)] + (pbf[par * 2 + 1] if cc == 0 else []),
                        inc=(pe if cc == NCC - 1 else None))
                v_pe = p.val(pe)
                slw.free = [(pe, v_pe)]
                gst = SBF[par]
                slgst = s_sbf[par]
                p.emit("scalar", lambda e, bg=bg, oc=oc, gst=gst: e.activation(
                    gst[:, :], bg[:, :], AF.Silu, scale=SGU[:, oc:oc + 1]),
                    waits=[(pe, v_pe)] + slgst.free, inc=act, dr=True)
                v_gst = p.val(act)
                pbf[par * 2] = [(act, v_gst)]
                ust = SBF[2 + par]
                slust = s_sbf[2 + par]
                p.emit("vector", lambda e, bu=bu, oc=oc, ust=ust: e.tensor_scalar_mul(
                    ust[:, :], bu[:, :], SGU[:, 64 + oc:65 + oc]),
                    waits=[(pe, v_pe)] + slust.free, inc=dve, dr=True)
                v_ust = p.val(dve)
                pbf[par * 2 + 1] = [(dve, v_ust)]
                p.emit("vector", lambda e, oc=oc, gst=gst, ust=ust: e.tensor_tensor(
                    ARA[:, oc * 512:(oc + 1) * 512], gst[:, :], ust[:, :], ALU.mult),
                    waits=[(act, v_gst)] + (s_ara.free if oc == 0 else []),
                    inc=dve, dr=True)
                slgst.free = [(dve, p.val(dve))]
                slust.free = [(dve, p.val(dve))]
            v_fuse = p.val(dve)
            v_pe_gu = p.val(pe)
            if debug and b == 0 and tt == 0:
                for oc_ in range(64):
                    p.emit("gpsimd", lambda e, oc_=oc_: e.dma_start(
                        dbg["dbg_fuse"][oc_, :, :], ARA[:, oc_ * 512:(oc_ + 1) * 512]),
                        waits=[(dve, v_fuse)], inc=dbgc)
            for o2 in range(16):
                par = o2 % 2
                dwv = WB[par][:, 0:8192]
                slw = s_wb[par]
                chw = p.dchan(f"wb{par}")
                p.emit("gpsimd", lambda e, o2=o2, dwv=dwv: e.dma_start(
                    dwv, w_down[o2].rearrange("p fc o -> p (fc o)")),
                    waits=slw.free + ([(pe, v_pe_gu)] if o2 < 2 else []), inc=chw)
                v_w = p.val(chw)
                bank = pb[4 + par]
                for fc2 in range(64):
                    p.emit("tensor", lambda e, fc2=fc2, dwv=dwv, bank=bank: e.matmul(
                        bank[:, :], dwv[:, fc2 * 128:(fc2 + 1) * 128],
                        ARA[:, fc2 * 512:(fc2 + 1) * 512],
                        start=(fc2 == 0), stop=(fc2 == 63), skip_group_check=True),
                        waits=[(chw, v_w), (dve, v_fuse)] + (pbf[4 + par] if fc2 == 0 else []),
                        inc=(pe if fc2 == 63 else None))
                v_pe = p.val(pe)
                slw.free = [(pe, v_pe)]
                xt = STG[par]
                slx = s_stg[par]
                chx = p.dchan(f"sx{par}")
                p.emit("sync", lambda e, o2=o2, tt=tt, xt=xt: e.dma_start(
                    xt[:, :], x1_scr[o2, :, tt * 512:(tt + 1) * 512]),
                    waits=slx.free + v_x1_done, inc=chx)
                v_x1t = p.val(chx)
                ost = STG[2 + par]
                slo = s_stg[2 + par]
                p.emit("vector", lambda e, bank=bank, o2=o2, xt=xt, ost=ost: e.scalar_tensor_tensor(
                    ost[:, :], bank[:, :], SM[:, 112 + o2:113 + o2], xt[:, :], ALU.mult, ALU.add),
                    waits=[(pe, v_pe), (chx, v_x1t)] + slo.free, inc=dve, dr=True)
                v_o = p.val(dve)
                pbf[4 + par] = [(dve, v_o)]
                slx.free = [(dve, v_o)]
                cho = p.dchan(f"so{par}")
                p.emit("sync", lambda e, b=b, o2=o2, tt=tt, ost=ost: e.dma_start(
                    out_fm[b, o2, :, tt * 512:(tt + 1) * 512], ost[:, :]),
                    waits=[(dve, v_o)], inc=cho)
                slo.free = [(cho, p.val(cho))]
            s_ara.free = [(pe, p.val(pe))]
        s_ln.free = [(pe, p.val(pe))]

    final_so = [("so0", p.val("so0")), ("so1", p.val("so1"))]

    semref = {ch: nc.alloc_semaphore(f"sem_{ch}") for ch in p.count}
    nc.all_engine_barrier()
    nums = sorted(s.num for s in semref.values())
    lo, hi = nums[0], nums[-1] + 1
    assert nums == list(range(lo, hi)), nums
    nc.gpsimd.dma_reset(range(lo, hi)) if hasattr(nc.gpsimd, "dma_reset") else None
    nc.gpsimd.sem_clear(range(lo, hi))
    nc.all_engine_barrier()
    if True:
        with nc.Block() as blk:
            def mk(name):
                def replay(eng):
                    for build, waits, inc, dr in p.ops[name]:
                        for ch, v in waits:
                            eng.wait_ge(semref[ch], v)
                        ins = build(eng)
                        if dr or (FORCE_DRAIN and inc in ("dve", "act")):
                            ins = eng.drain(fusable=False)
                        if inc is not None:
                            ins.then_inc(semref[inc], p.step[inc])
                    if name == "sync":
                        for chn, vv in final_so:
                            eng.wait_ge(semref[chn], vv)
                    if name == "gpsimd" and p.val("dbgc"):
                        eng.wait_ge(semref["dbgc"], p.val("dbgc"))
                return replay
            blk.sync(mk("sync"))
            blk.scalar(mk("scalar"))
            blk.vector(mk("vector"))
            blk.tensor(mk("tensor"))
            blk.gpsimd(mk("gpsimd"))
    return nc


def hs_c(hh):
    return slice(64 * hh, 64 * hh + 64)



def _prep_inputs(inputs):
    x = np.asarray(inputs["L0_attn_x"])

    def colfm(v):
        return np.asarray(v, np.float32).reshape(-1).reshape(16, 128).T

    def tile_lhsT(wt, n_ocol, osz):
        wt = np.asarray(wt)
        a = wt.reshape(wt.shape[0] // 128, 128, n_ocol, osz)
        return np.ascontiguousarray(a.transpose(2, 1, 0, 3))

    smalls = np.zeros((128, 176), np.float32)
    smalls[:, 0:16] = colfm(inputs["L0_attn_time_mix_k"])
    smalls[:, 16:32] = colfm(inputs["L0_attn_time_mix_v"])
    smalls[:, 32:48] = colfm(inputs["L0_attn_time_mix_r"])
    smalls[:, 48:64] = colfm(inputs["L0_attn_time_mix_g"])
    smalls[:, 64:80] = colfm(inputs["L0_attn_time_decay"])
    smalls[:, 80:96] = colfm(inputs["L0_attn_ln_w"])
    smalls[:, 96:112] = colfm(inputs["L0_attn_ln_b"])
    smalls[:, 112:128] = colfm(inputs["L0_ffn_s_down"])
    smalls[:, 128:144] = colfm(inputs["L0_attn_state"])
    smalls[:, 144:160] = colfm(inputs["L0_attn_time_faaaa"])
    sgu = np.zeros((128, 128), np.float32)
    sgu[:, 0:64] = np.asarray(inputs["L0_ffn_s_gate"], np.float32).reshape(64, 128).T
    sgu[:, 64:128] = np.asarray(inputs["L0_ffn_s_up"], np.float32).reshape(64, 128).T

    com = {
        "w_r": tile_lhsT(np.asarray(inputs["L0_attn_w_r"], np.float32).T, 16, 128),
        "w_k": tile_lhsT(np.asarray(inputs["L0_attn_w_k"], np.float32).T, 16, 128),
        "w_g": tile_lhsT(np.asarray(inputs["L0_attn_w_g"], np.float32).T, 16, 128),
        "w_o": tile_lhsT(np.asarray(inputs["L0_attn_w_o"], np.float32).T, 16, 128),
        "w_v": tile_lhsT(np.asarray(inputs["L0_attn_w_v"], np.float32).T, 4, 512),
        "w_gate": tile_lhsT(np.asarray(inputs["L0_ffn_w_gate"]), 64, 128),
        "w_up": tile_lhsT(np.asarray(inputs["L0_ffn_w_up"]), 64, 128),
        "w_down": tile_lhsT(np.asarray(inputs["L0_ffn_w_down"]), 16, 128),
        "smalls": smalls,
        "sgu": sgu,
        "wkv0": np.ascontiguousarray(np.asarray(inputs["L0_attn_wkvstate"], np.float32)),
        "iota_d": np.ascontiguousarray(np.broadcast_to(np.arange(T, dtype=np.float32), (128, T))),
        "ones_cb": np.ones((128, 1), ml_dtypes.bfloat16),
        "ones_rf": np.ones((1, 128), np.float32),
        "maskS": np.ascontiguousarray(np.triu(np.ones((128, 128), np.float32), 1)),
        "b2c": np.ascontiguousarray(np.kron(np.eye(2, dtype=np.float32), np.ones((64, 1), np.float32))),
    }
    maps = []
    for ci in range(NCORES):
        xs = x[ci * B_LOC:(ci + 1) * B_LOC]
        m = dict(com)
        m["x_fm"] = np.ascontiguousarray(xs.transpose(0, 2, 1)).reshape(B_LOC, NCC, 128, T)
        maps.append(m)
    return maps


_cache = {}


def kernel(**inputs):
    if "nc" not in _cache:
        _cache["nc"] = build_program(debug=DEBUG)
    nc = _cache["nc"]
    maps = _prep_inputs(inputs)
    res = run_bass_kernel_spmd(nc, maps, core_ids=list(range(NCORES)))
    _cache["res"] = res
    outs = []
    for ci in range(NCORES):
        o = res.results[ci]["out_fm"].reshape(B_LOC, C, T)
        outs.append(np.ascontiguousarray(o.transpose(0, 2, 1)))
    return np.concatenate(outs, axis=0)
